# version 09
# baseline (speedup 1.0000x reference)
import numpy as np
import ml_dtypes
from contextlib import ExitStack

import concourse.bass as bass
import concourse.mybir as mybir
from concourse.bass_utils import run_bass_kernel_spmd

F32 = mybir.dt.float32
BF16 = mybir.dt.bfloat16
AF = mybir.ActivationFunctionType
ALU = mybir.AluOpType
AX = mybir.AxisListType

D = 2048
S = 4096
TOK = 1024
NH = 16
G = 2
HD = 128
DCH = 16
NKT = 32
NE = 16
NJ = 11
SJ = 22
ENGINES = ("sync", "gpsimd", "tensor", "vector", "scalar")
KB = 1024


class P:

    def __init__(self):
        self.counts = {}
        self.active = None
        self.eng = None
        self.sems = []

    def sem(self, handle):
        self.counts[id(handle)] = 0
        self.sems.append(handle)
        return handle

    def reset(self, name, eng):
        self.active = name
        self.eng = eng
        for k in self.counts:
            self.counts[k] = 0

    def do(self, engine, thunk, inc=None):
        if self.active == engine:
            inst = thunk(self.eng)
            self.last_inst = inst
            if inc is not None:
                for s, n in inc:
                    inst.then_inc(s, n)
        if inc is not None:
            for s, n in inc:
                self.counts[id(s)] += n

    def mark(self, engine, sem, n=1):
        if self.active == engine:
            self.last_inst.then_inc(sem, n)
        self.counts[id(sem)] += n

    def wait(self, engine, sem, val=None):
        tgt = self.counts[id(sem)] if val is None else val
        if tgt <= 0:
            return
        if self.active == engine:
            self.eng.wait_ge(sem, tgt)

    def count(self, sem):
        return self.counts[id(sem)]

    def drain(self, engine):
        if self.active == engine:
            self.eng.drain()


def build_program(stop_after=None):
    nc = bass.Bass("TRN2", target_bir_lowering=False, detect_race_conditions=False)

    def din(name, shape, dt=F32):
        return nc.dram_tensor(name, list(shape), dt, kind="ExternalInput").ap()

    x_batch = din("x_batch", [D, S])
    x_own = din("x_own", [D, TOK])
    wq_r = din("wq_r", [NH, 128, DCH, 128])
    wk_r = din("wk_r", [G, 128, DCH, 128])
    wv_r = din("wv_r", [G, 128, DCH, 128])
    wo_r = din("wo_r", [DCH, 128, DCH, 128])
    bq_r = din("bq_r", [128, NH])
    bk_r = din("bk_r", [128, G])
    bv_row = din("bv_row", [1, G * 128])
    bo_r = din("bo_r", [128, DCH])
    gwt = din("gwt", [D, NE])
    wgu_r = din("wgu_r", [NE, NJ, 2, 128, DCH, 128])
    wd_r = din("wd_r", [NE, 8, 2, 128, NJ, 128])
    sgu_r = din("sgu_r", [SJ, 2, 128, DCH, 128])
    sd_r = din("sd_r", [8, 2, 128, SJ, 128])
    cosq = din("cosq", [128, TOK], BF16)
    sinq = din("sinq", [128, TOK], BF16)
    cosk = din("cosk", [128, S], BF16)
    sink = din("sink", [128, S], BF16)
    idf = din("idf", [128, 128])
    sel = din("sel", [NE, NE * 128])

    out = nc.dram_tensor("out", [D, TOK], F32, kind="ExternalOutput").ap()

    ARENA = 202 * KB
    arena = nc.alloc_sbuf_tensor("arena", [128, ARENA], mybir.dt.uint8)
    base = nc.lookup_mloc(arena).addr

    def at(name, shape, dt, off):
        assert off % 32 == 0, name
        need = int(np.prod(shape[1:])) * mybir.dt.size(dt)
        assert off + need <= ARENA, f"{name}: {off}+{need} > {ARENA}"
        return nc.alloc_sbuf_tensor_at(name, list(shape), dt, offset=base + off).ap()

    ones_bf = at("ones_bf", [128, 1], BF16, 0)
    ones1f = at("ones1f", [1, 128], F32, 32)
    ones1b = at("ones1b", [1, 128], BF16, 576)
    onesf = at("onesf", [128, 1], F32, 864)
    epsv = at("epsv", [1, 1], F32, 896)
    bq_sb = at("bq_sb", [128, NH], F32, 928)
    bk_sb = at("bk_sb", [128, G], F32, 992)
    bv_bf = at("bv_bf", [1, G * 128], BF16, 1024)
    bo_sb = at("bo_sb", [128, DCH], F32, 1568)
    qmax = at("qmax", [1, NH], F32, 1632)
    kmax = at("kmax", [1, G], F32, 1696)
    negc = at("negc", [128, NH], F32, 1760)
    gwt_sb = at("gwt_sb", [128, DCH * NE], F32, 1824)
    idf_sb = at("idf_sb", [128, 128], F32, 2848)
    comb_sb = at("comb_sb", [16, 8 * 128], F32, 3392)
    sel_sb = at("sel_sb", [NE, NE * 128], F32, 7488)
    sc1 = at("sc1", [1, 1], F32, 15680)
    qt = at("qt", [128, NH * TOK], BF16, 16 * KB)
    x1b = at("x1b", [128, DCH * TOK], BF16, 16 * KB)
    xb_bf = [at("xb_bf0", [128, DCH * 512], BF16, 48 * KB),
             at("xb_bf1", [128, DCH * 512], BF16, 64 * KB)]
    ktb16 = at("ktb16", [128, G * S], BF16, 144 * KB)
    cosk_sb = at("cosk_sb", [128, S], BF16, 168 * KB)
    sink_sb = at("sink_sb", [128, S], BF16, 176 * KB)
    x_own_bf = at("x_own_bf", [128, DCH * TOK], BF16, 48 * KB)
    wq_bf = [at("wq_bf0", [128, D], BF16, 80 * KB),
             at("wq_bf1", [128, D], BF16, 84 * KB)]
    qf = [at("qf0", [128, TOK], F32, 88 * KB),
          at("qf1", [128, TOK], F32, 92 * KB)]
    qrot = [at("qrot0", [128, TOK], F32, 96 * KB),
            at("qrot1", [128, TOK], F32, 100 * KB)]
    krot = [at("krot0", [128, TOK], BF16, 160 * KB),
            at("krot1", [128, TOK], BF16, 162 * KB)]
    t1_t = at("t1_t", [128, TOK], F32, 104 * KB)
    t2_t = at("t2_t", [128, TOK], F32, 108 * KB)
    et = [at("et%d" % i, [128, TOK], BF16, (88 + 2 * i) * KB)
          for i in range(12)]
    invsb = at("invsb", [128, TOK], F32, 152 * KB)
    rspart = at("rspart", [128, TOK], F32, 196 * KB)
    x1 = at("x1", [128, DCH * TOK], F32, 48 * KB)
    vn = at("vn", [128, G * S], BF16, 112 * KB)
    ktbf = at("ktbf", [128, G * S], BF16, 128 * KB)
    wo_bf = [at("wo_bf0", [128, D], BF16, 112 * KB),
             at("wo_bf1", [128, D], BF16, 116 * KB)]
    tmp_f = [at("tmp_f0", [128, TOK], F32, 120 * KB),
             at("tmp_f1", [128, TOK], F32, 124 * KB)]
    xo_f = [at("xo_f0", [128, TOK], F32, 128 * KB),
            at("xo_f1", [128, TOK], F32, 132 * KB)]
    sq_t = [at("sq_t0", [128, TOK], BF16, 136 * KB),
            at("sq_t1", [128, TOK], BF16, 138 * KB)]
    xc_bf = [at("xc_bf0", [128, TOK], BF16, 140 * KB),
             at("xc_bf1", [128, TOK], BF16, 142 * KB)]
    lg_sb = at("lg_sb", [16, TOK], F32, 144 * KB)
    lgt_t = at("lgt_t", [128, 16], F32, 148 * KB)
    lgt8 = at("lgt8", [128, 8 * 16], F32, 148 * KB + 512)
    nm18 = at("nm18", [128, 8 * 16], F32, 149 * KB)
    cmb8 = at("cmb8", [128, 8 * 16], F32, 149 * KB + 512)
    m1c = at("m1c", [128, 1], F32, 150 * KB)
    nm1 = at("nm1", [128, 1], F32, 150 * KB + 64)
    rsum_t = at("rsum_t", [128, 1], F32, 150 * KB + 128)
    rinv_t = at("rinv_t", [128, 1], F32, 150 * KB + 192)
    m2c = at("m2c", [128, 1], F32, 150 * KB + 256)
    esc_t = at("esc_t", [128, 16], F32, 150 * KB + 320)
    sc_t = at("sc_t", [128, 16], F32, 150 * KB + 384)
    eqm_t = at("eqm_t", [128, 16], F32, 150 * KB + 448)
    big_t = at("big_t", [128, 16], F32, 150 * KB + 512)
    msk_t = at("msk_t", [128, 16], F32, 150 * KB + 576)
    keep_t = at("keep_t", [128, 16], F32, 150 * KB + 640)
    cmb_t = at("cmb_t", [128, 16], F32, 150 * KB + 704)
    wgu_bf = [at("wgu_bf0", [128, 2 * D], BF16, 112 * KB),
              at("wgu_bf1", [128, 2 * D], BF16, 120 * KB)]
    wd_bf = [at("wd_bf0", [128, 2 * NJ * 128], BF16, 128 * KB),
             at("wd_bf1", [128, 2 * NJ * 128], BF16, 134 * KB)]
    tmp7 = [at("tmp7_0", [128, TOK], F32, 112 * KB),
            at("tmp7_1", [128, TOK], F32, 116 * KB)]
    wk_bf = at("wk_bf", [128, G * D], BF16, 184 * KB)
    wv_bf = at("wv_bf", [128, G * D], BF16, 192 * KB)
    sqk_bf = [at("sqk_bf0", [128, TOK], BF16, 164 * KB),
              at("sqk_bf1", [128, TOK], BF16, 166 * KB)]
    sq_bf = [at("sq_bf0", [128, TOK], BF16, 184 * KB),
             at("sq_bf1", [128, TOK], BF16, 186 * KB),
             at("sq_bf2", [128, TOK], BF16, 188 * KB),
             at("sq_bf3", [128, TOK], BF16, 194 * KB)]
    cosq_sb = at("cosq_sb", [128, TOK], BF16, 190 * KB)
    sinq_sb = at("sinq_sb", [128, TOK], BF16, 192 * KB)
    aoT = at("aoT", [128, NH * TOK], BF16, 160 * KB)
    a_sb = at("a_sb", [128, TOK], F32, 144 * KB)
    b_sb = at("b_sb", [128, TOK], F32, 148 * KB)
    lt1 = at("lt1b", [1, TOK], F32, 152 * KB)
    lt2 = at("lt2b", [1, TOK], F32, 156 * KB)
    lt3 = at("lt3b", [1, TOK], F32, 192 * KB)
    hpr = at("hpr", [128, NJ * TOK], BF16, 160 * KB)
    sg_bf = [at("sg_bf0", [128, TOK], BF16, 182 * KB),
             at("sg_bf1", [128, TOK], BF16, 184 * KB)]
    uf_sb = [at("uf_sb0", [128, TOK], F32, 186 * KB),
             at("uf_sb1", [128, TOK], F32, 190 * KB)]
    ce_sb = at("ce_sb", [128, TOK], F32, 194 * KB)

    es = ExitStack()
    with es:
        PS = es.enter_context(nc.psum_tensor("PS", [128, 4096], F32))

        sems = {}

        def mksem(name):
            sems[name] = es.enter_context(nc.semaphore(name))
            return sems[name]

        s_te = mksem("s_te")
        s_ve = mksem("s_ve")
        s_se = mksem("s_se")
        s_st = mksem("s_st")
        s_const = mksem("s_const")
        s_kv = mksem("s_kv")
        s_xb = [mksem("s_xb0"), mksem("s_xb1")]
        s_qr = [mksem("s_qr0"), mksem("s_qr1")]
        s_kr = mksem("s_kr")
        s_xobf = mksem("s_xobf")
        s_wq = [mksem("s_wq0"), mksem("s_wq1")]
        s_xof = [mksem("s_xof0"), mksem("s_xof1")]
        s_wg = [mksem("s_wg0"), mksem("s_wg1")]
        s_wd = [mksem("s_wd0"), mksem("s_wd1")]

        blk = es.enter_context(nc.Block())
        pp = P()
        for s in sems.values():
            pp.sem(s)

        def emit(p):
            W, C, DO, DR = p.wait, p.count, p.do, p.drain

            def bank(b, w=512):
                return PS[:, b * 512:b * 512 + w]

            def temark(eng="tensor"):
                p.mark(eng, s_te)
                return C(s_te)

            def vemark():
                DR("vector")
                DO("vector", lambda e: e.engine_nop(), inc=[(s_ve, 1)])
                return C(s_ve)

            def barrier():
                for ename in ENGINES:
                    for s in pp.sems:
                        if s is not s_st:
                            W(ename, s)

            def maybe_stop(name):
                if stop_after != name:
                    return False
                barrier()
                W("sync", s_ve, C(s_ve))
                DO("sync", lambda e: e.dma_start(
                    out.rearrange("(dc p) t -> p dc t", p=128),
                    x1[:, :].rearrange("p (dc t) -> p dc t", dc=DCH)),
                   inc=[(s_st, 16)])
                W("sync", s_st, C(s_st))
                return True

            for dst, src in ((bq_sb, bq_r), (bk_sb, bk_r), (bo_sb, bo_r),
                             (idf_sb, idf), (sel_sb, sel),
                             (cosk_sb, cosk), (sink_sb, sink)):
                DO("sync", lambda e, d=dst, s=src: e.dma_start(d[:, :], s),
                   inc=[(s_const, 16)])
            DO("sync", lambda e: e.dma_start(
                gwt_sb[:, :].rearrange("p (h e2) -> p h e2", h=DCH),
                gwt.rearrange("(h p) e2 -> p h e2", p=128)), inc=[(s_const, 16)])
            const_tot = C(s_const)
            DO("gpsimd", lambda e: e.dma_start(bv_bf[:, :], bv_row), inc=[(s_kv, 16)])
            DO("gpsimd", lambda e: e.dma_start(
                wk_bf[:, :].rearrange("p (g f) -> p g f", g=G),
                wk_r.rearrange("g p h c -> p g (h c)")), inc=[(s_kv, 16)])
            DO("gpsimd", lambda e: e.dma_start(
                wv_bf[:, :].rearrange("p (g f) -> p g f", g=G),
                wv_r.rearrange("g p h c -> p g (h c)")), inc=[(s_kv, 16)])
            kv_tot = C(s_kv)
            for buf, val in ((ones_bf, 1.0), (ones1f, 1.0), (ones1b, 1.0),
                             (onesf, 1.0), (epsv, 1e-5), (qmax, 0.0),
                             (kmax, 0.0)):
                DO("vector", lambda e, b=buf, v=val: e.memset(b[:, :], v))
            ve_init = vemark()

            if maybe_stop("ph0"):
                return
            te_k, te_v, te_xbfree, se_kvev = {}, {}, {}, {}
            for ch in range(8):
                sl = ch % 2
                if ch >= 2:
                    W("gpsimd", s_te, te_xbfree[ch - 2])
                DO("gpsimd", lambda e, ch=ch, sl=sl: e.dma_start(
                    xb_bf[sl][:, :].rearrange("p (h t) -> p h t", h=DCH),
                    x_batch[:, ch * 512:(ch + 1) * 512]
                    .rearrange("(h p) t -> p h t", p=128)), inc=[(s_xb[sl], 16)])
                W("tensor", s_xb[sl], C(s_xb[sl]))
                if ch == 0:
                    W("tensor", s_kv, kv_tot)
                    W("tensor", s_ve, ve_init)
                for g in range(G):
                    kb, vb = g * 2 + sl, 4 + g * 2 + sl
                    if ch >= 2:
                        W("tensor", s_se, se_kvev[(ch - 2, g, 0)])
                    for hc in range(DCH):
                        DO("tensor", lambda e, g=g, hc=hc, sl=sl, kb=kb: e.matmul(
                            bank(kb),
                            wk_bf[:, g * D + hc * 128:g * D + hc * 128 + 128],
                            xb_bf[sl][:, hc * 512:(hc + 1) * 512],
                            start=(hc == 0), stop=(hc == DCH - 1)))
                    te_k[(ch, g)] = temark()
                    if ch >= 2:
                        W("tensor", s_se, se_kvev[(ch - 2, g, 1)])
                    for tt in range(4):
                        for hc in range(DCH):
                            DO("tensor", lambda e, g=g, hc=hc, tt=tt, sl=sl, vb=vb: e.matmul(
                                PS[:, vb * 512 + tt * 128:vb * 512 + tt * 128 + 128],
                                xb_bf[sl][:, hc * 512 + tt * 128:hc * 512 + tt * 128 + 128],
                                wv_bf[:, g * D + hc * 128:g * D + hc * 128 + 128],
                                start=(hc == 0), stop=False))
                        DO("tensor", lambda e, g=g, tt=tt, vb=vb: e.matmul(
                            PS[:, vb * 512 + tt * 128:vb * 512 + tt * 128 + 128],
                            ones1b[:, :], bv_bf[0:1, g * 128:(g + 1) * 128],
                            start=False, stop=True))
                    te_v[(ch, g)] = temark()
                te_xbfree[ch] = C(s_te)
                for g in range(G):
                    kb, vb = g * 2 + sl, 4 + g * 2 + sl
                    W("scalar", s_te, te_k[(ch, g)])
                    DO("scalar", lambda e, g=g, ch=ch, kb=kb: e.activation(
                        ktb16[:, g * S + ch * 512:g * S + (ch + 1) * 512],
                        bank(kb), AF.Identity, bias=bk_sb[:, g:g + 1]),
                       inc=[(s_se, 1)])
                    se_kvev[(ch, g, 0)] = C(s_se)
                    W("scalar", s_te, te_v[(ch, g)])
                    DO("scalar", lambda e, g=g, ch=ch, vb=vb: e.copy(
                        vn[:, (g * NKT + ch * 4) * 128:(g * NKT + ch * 4) * 128 + 512],
                        bank(vb)), inc=[(s_se, 1)])
                    se_kvev[(ch, g, 1)] = C(s_se)

            if maybe_stop("ph1b"):
                return
            W("gpsimd", s_te, te_xbfree[6])
            DO("gpsimd", lambda e: e.dma_start(
                x_own_bf[:, 0:8 * TOK].rearrange("p (h t) -> p h t", h=8),
                x_own[0:1024, :].rearrange("(h p) t -> p h t", p=128)),
               inc=[(s_xobf, 16)])
            W("gpsimd", s_te, te_xbfree[7])
            DO("gpsimd", lambda e: e.dma_start(
                x_own_bf[:, 8 * TOK:16 * TOK].rearrange("p (h t) -> p h t", h=8),
                x_own[1024:2048, :].rearrange("(h p) t -> p h t", p=128)),
               inc=[(s_xobf, 16)])
            xobf_tot = C(s_xobf)
            barrier()
            DO("sync", lambda e: e.dma_start(cosq_sb[:, :], cosq),
               inc=[(s_const, 16)])
            DO("sync", lambda e: e.dma_start(sinq_sb[:, :], sinq),
               inc=[(s_const, 16)])
            te_q, se_q, ve_q, ve_qn, te_qn_map = {}, {}, {}, {}, {}
            ve_kn, te_kn_map, ve_krope = {}, {}, {}
            se_kv_last = C(s_se)

            def emit_qn_pe(hq):
                W("tensor", s_ve, ve_q[hq])
                if hq >= 1:
                    W("tensor", s_ve, ve_qn[hq - 1])
                for half in range(2):
                    DO("tensor", lambda e, half=half, hq=hq: e.matmul(
                        PS[0:1, 2048 + half * 512:2048 + half * 512 + 512],
                        ones_bf[:, :],
                        sq_bf[hq % 4][:, half * 512:half * 512 + 512],
                        start=True, stop=True))
                te_qn_map[hq] = temark()

            def emit_qn_dve(hq):
                W("vector", s_te, te_qn_map[hq])
                DO("vector", lambda e: e.reduce_max(
                    sc1[0:1, 0:1], PS[0:1, 2048:3072], axis=AX.X))
                DR("vector")
                DO("vector", lambda e, hq=hq: e.tensor_copy(
                    qmax[0:1, hq:hq + 1], sc1[0:1, 0:1]))
                ve_qn[hq] = vemark()

            def emit_kn_pe(s):
                W("tensor", s_ve, ve_krope[s])
                if s >= 1:
                    W("tensor", s_ve, ve_kn[s - 1])
                for half in range(2):
                    DO("tensor", lambda e, half=half, s=s: e.matmul(
                        PS[0:1, 3072 + half * 512:3072 + half * 512 + 512],
                        ones_bf[:, :],
                        sqk_bf[s % 2][:, half * 512:half * 512 + 512],
                        start=True, stop=True))
                te_kn_map[s] = temark()

            def emit_kn_dve(s):
                W("vector", s_te, te_kn_map[s])
                DO("vector", lambda e: e.reduce_max(
                    sc1[0:1, 0:1], PS[0:1, 3072:4096], axis=AX.X))
                DR("vector")
                DO("vector", lambda e, s=s: e.tensor_tensor(
                    kmax[0:1, s // 4:s // 4 + 1], kmax[0:1, s // 4:s // 4 + 1],
                    sc1[0:1, 0:1], ALU.max))
                ve_kn[s] = vemark()

            for h in range(NH):
                sl = h % 2
                ks = h
                if h >= 2:
                    W("gpsimd", s_te, te_q[h - 2])
                DO("gpsimd", lambda e, h=h, sl=sl: e.dma_start(
                    wq_bf[sl][:, :].rearrange("p (hc c) -> p hc c", hc=DCH),
                    wq_r[h]), inc=[(s_wq[sl], 16)])
                W("tensor", s_wq[sl], C(s_wq[sl]))
                if h == 0:
                    W("tensor", s_xobf, xobf_tot)
                if h >= 2:
                    W("tensor", s_se, se_q[h - 2])
                for hc in range(DCH):
                    for half in range(2):
                        DO("tensor", lambda e, hc=hc, half=half, sl=sl: e.matmul(
                            PS[:, sl * 1024 + half * 512:sl * 1024 + half * 512 + 512],
                            wq_bf[sl][:, hc * 128:(hc + 1) * 128],
                            x_own_bf[:, hc * 1024 + half * 512:hc * 1024 + half * 512 + 512],
                            start=(hc == 0), stop=(hc == DCH - 1)))
                te_q[h] = temark()
                if h >= 3:
                    emit_qn_pe(h - 3)
                if h >= 1 and (h - 1) < 8:
                    emit_kn_pe(h - 1)
                W("scalar", s_te, te_q[h])
                if h >= 2:
                    W("scalar", s_ve, ve_q[h - 2])
                DO("scalar", lambda e, h=h, sl=sl: e.activation(
                    qf[sl][:, :], PS[:, sl * 1024:sl * 1024 + 1024],
                    AF.Identity, bias=bq_sb[:, h:h + 1]), inc=[(s_se, 1)])
                se_q[h] = C(s_se)
                W("sync", s_se, se_q[h])
                if h >= 2:
                    W("sync", s_ve, ve_q[h - 2])
                DO("sync", lambda e, sl=sl: e.dma_start(
                    qrot[sl][0:64, :], qf[sl][64:128, :]), inc=[(s_qr[sl], 16)])
                DO("sync", lambda e, sl=sl: e.dma_start(
                    qrot[sl][64:128, :], qf[sl][0:64, :]), inc=[(s_qr[sl], 16)])
                if ks < 8:
                    g, slc = ks // 4, ks % 4
                    cb = g * S + slc * 1024
                    if ks >= 2:
                        W("sync", s_ve, ve_krope[ks - 2])
                    DO("sync", lambda e, cb=cb, ks=ks: e.dma_start(
                        krot[ks % 2][0:64, :], ktb16[64:128, cb:cb + 1024]),
                       inc=[(s_kr, 16)])
                    DO("sync", lambda e, cb=cb, ks=ks: e.dma_start(
                        krot[ks % 2][64:128, :], ktb16[0:64, cb:cb + 1024]),
                       inc=[(s_kr, 16)])
                kr_cnt = C(s_kr)
                W("vector", s_qr[sl], C(s_qr[sl]))
                W("vector", s_se, se_q[h])
                if h == 0:
                    W("vector", s_const, C(s_const))
                DO("vector", lambda e, sl=sl: e.tensor_tensor(
                    t1_t[:, :], qf[sl][:, :], cosq_sb[:, :], ALU.mult))
                DO("vector", lambda e, sl=sl: e.tensor_tensor(
                    t2_t[:, :], qrot[sl][:, :], sinq_sb[:, :], ALU.mult))
                DR("vector")
                DO("vector", lambda e, h=h: e.tensor_tensor(
                    qt[:, h * TOK:(h + 1) * TOK], t1_t[:, :], t2_t[:, :],
                    ALU.add))
                DR("vector")
                if h >= 4:
                    W("vector", s_te, te_qn_map[h - 4])
                DO("vector", lambda e, h=h: e.tensor_tensor(
                    sq_bf[h % 4][:, :], qt[:, h * TOK:(h + 1) * TOK],
                    qt[:, h * TOK:(h + 1) * TOK], ALU.mult))
                ve_q[h] = vemark()
                if ks < 8:
                    g, slc = ks // 4, ks % 4
                    cb = g * S + slc * 1024
                    W("vector", s_kr, kr_cnt)
                    if h == 0:
                        W("vector", s_const, C(s_const))
                        W("vector", s_se, se_kv_last)
                    DO("vector", lambda e, cb=cb, slc=slc: e.tensor_tensor(
                        t1_t[:, :], ktb16[:, cb:cb + 1024],
                        cosk_sb[:, slc * 1024:slc * 1024 + 1024], ALU.mult))
                    DO("vector", lambda e, slc=slc, ks=ks: e.tensor_tensor(
                        t2_t[:, :], krot[ks % 2][:, :],
                        sink_sb[:, slc * 1024:slc * 1024 + 1024], ALU.mult))
                    DR("vector")
                    DO("vector", lambda e, cb=cb: e.tensor_tensor(
                        ktbf[:, cb:cb + 1024], t1_t[:, :], t2_t[:, :], ALU.add))
                    DR("vector")
                    if ks >= 2:
                        W("vector", s_te, te_kn_map[ks - 2])
                    DO("vector", lambda e, cb=cb, ks=ks: e.tensor_tensor(
                        sqk_bf[ks % 2][:, :], ktbf[:, cb:cb + 1024],
                        ktbf[:, cb:cb + 1024], ALU.mult))
                    ve_krope[ks] = vemark()
                if h >= 3:
                    emit_qn_dve(h - 3)
                if h >= 1 and (h - 1) < 8:
                    emit_kn_dve(h - 1)
            for hq in (NH - 3, NH - 2, NH - 1):
                emit_qn_pe(hq)
                emit_qn_dve(hq)

            DR("vector")
            for g in range(G):
                DO("vector", lambda e, g=g: e.tensor_scalar(
                    t1_t[0:1, g * 8:g * 8 + 8], qmax[0:1, g * 8:g * 8 + 8],
                    kmax[0:1, g:g + 1], None, ALU.mult))
            ve_prod = vemark()
            W("scalar", s_ve, ve_prod)
            DO("scalar", lambda e: e.activation(
                t2_t[0:1, 0:16], t1_t[0:1, 0:16], AF.Ln))
            DR("scalar")
            DO("scalar", lambda e: e.activation(
                t2_t[0:1, 16:32], t2_t[0:1, 0:16], AF.Exp, scale=0.5),
               inc=[(s_se, 1)])
            se_cst = C(s_se)
            W("vector", s_se, se_cst)
            DO("vector", lambda e: e.tensor_scalar(
                t2_t[0:1, 32:48], t2_t[0:1, 16:32], -(1.03 / 128.0), None,
                ALU.mult))
            ve_ncst = vemark()
            W("tensor", s_ve, ve_ncst)
            DO("tensor", lambda e: e.matmul(
                PS[:, 0:16], ones1f[:, :], t2_t[0:1, 32:48], start=True,
                stop=True))
            te_negc = temark()
            W("vector", s_te, te_negc)
            DO("vector", lambda e: e.tensor_copy(negc[:, :], PS[:, 0:16]))
            ve_negc = vemark()

            barrier()
            PL = 10

            def slot(kt):
                return kt if kt < 2 else 2 + (kt - 2) % 10

            ve_out = {}
            ve_invs_prev = 0
            for h in range(NH):
                g = h // 8
                te_qk, te_pv = {}, {}
                se0 = C(s_se)
                se_exp = {kt: se0 + kt + 1 for kt in range(NKT)}
                ve0 = C(s_ve)
                dve_set = list(range(1, NKT))
                ve_rs = {kt: ve0 + kt for kt in range(NKT)}
                ve_rs[0] = ve0
                spair = (0, 1024, 3072)
                for kt in range(NKT + PL):
                    if kt < NKT:
                        if kt == 2 and h >= 1:
                            W("tensor", s_ve, ve_invs_prev)
                        if kt >= 3:
                            W("tensor", s_se, se_exp[kt - 3])
                        for half in range(2):
                            DO("tensor", lambda e, h=h, g=g, kt=kt, half=half: e.matmul(
                                PS[:, spair[kt % 3] + half * 512:spair[kt % 3] + half * 512 + 512],
                                ktbf[:, g * S + kt * 128:g * S + kt * 128 + 128],
                                qt[:, h * TOK + half * 512:h * TOK + half * 512 + 512],
                                start=True, stop=True))
                        te_qk[kt] = temark()
                    k2 = kt - PL
                    if k2 >= 0:
                        if kt >= NKT:
                            W("tensor", s_se, se_exp[k2])
                        if k2 == 0 and h >= 1:
                            W("tensor", s_ve, ve_out[h - 1])
                        for half in range(2):
                            DO("tensor", lambda e, g=g, k2=k2, half=half: e.matmul(
                                PS[:, 2048 + half * 512:2048 + half * 512 + 512],
                                vn[:, (g * NKT + k2) * 128:(g * NKT + k2) * 128 + 128],
                                et[slot(k2)][:, half * 512:half * 512 + 512],
                                start=(k2 == 0), stop=(k2 == NKT - 1)))
                        te_pv[k2] = temark()
                for kt in range(NKT):
                    W("scalar", s_te, te_pv[kt - PL] if kt >= PL else te_qk[kt])
                    if kt >= PL and (kt - PL) >= 1:
                        W("scalar", s_ve, ve_rs[kt - PL])
                    DO("scalar", lambda e, h=h, kt=kt: e.activation(
                        et[slot(kt)][:, :],
                        PS[:, spair[kt % 3]:spair[kt % 3] + 1024],
                        AF.Exp, bias=negc[:, h:h + 1], scale=1.0 / 128.0),
                       inc=[(s_se, 1)])
                    assert se_exp[kt] == C(s_se)
                W("vector", s_se, se_exp[0])
                DO("vector", lambda e: e.tensor_copy(rspart[:, :], et[0][:, :]))
                DR("vector")
                for kt in dve_set:
                    W("vector", s_se, se_exp[kt])
                    DO("vector", lambda e, kt=kt: e.tensor_tensor(
                        rspart[:, :], rspart[:, :], et[slot(kt)][:, :],
                        ALU.add))
                    DR("vector")
                    DO("vector", lambda e: e.engine_nop(), inc=[(s_ve, 1)])
                    assert ve_rs[kt] == C(s_ve)
                W("tensor", s_ve, ve_rs[NKT - 1])
                W("tensor", s_se, se_exp[NKT - 1])
                for half in range(2):
                    DO("tensor", lambda e, half=half: e.matmul(
                        PS[0:1, 3072 + half * 512:3072 + half * 512 + 512],
                        onesf[:, :], rspart[:, half * 512:half * 512 + 512],
                        start=True, stop=True))
                te_rs = temark()
                W("scalar", s_te, te_rs)
                DO("scalar", lambda e: e.activation(
                    rspart[0:1, 0:1024], PS[0:1, 3072:4096], AF.Ln))
                DR("scalar")
                DO("scalar", lambda e: e.activation(
                    invsb[0:1, 0:1024], rspart[0:1, 0:1024], AF.Exp,
                    scale=-1.0), inc=[(s_se, 1)])
                se_inv = C(s_se)
                W("tensor", s_se, se_inv)
                for half in range(2):
                    DO("tensor", lambda e, half=half: e.matmul(
                        PS[:, 3072 + half * 512:3072 + half * 512 + 512],
                        ones1f[:, :],
                        invsb[0:1, half * 512:half * 512 + 512],
                        start=True, stop=True))
                te_bc = temark()
                W("vector", s_te, te_bc)
                DO("vector", lambda e: e.tensor_copy(rspart[:, :], PS[:, 3072:4096]))
                ve_invs_prev = vemark()
                DR("vector")
                DO("vector", lambda e, h=h: e.tensor_tensor(
                    aoT[:, h * TOK:(h + 1) * TOK], PS[:, 2048:3072],
                    rspart[:, :], ALU.mult))
                ve_out[h] = vemark()

            barrier()
            te_wo, se_wo, ve_x1, te_st3 = {}, {}, {}, {}
            for dt in range(DCH):
                sl = dt % 2
                if dt >= 2:
                    W("gpsimd", s_te, te_wo[dt - 2])
                DO("gpsimd", lambda e, dt=dt, sl=sl: e.dma_start(
                    wo_bf[sl][:, :].rearrange("p (hc c) -> p hc c", hc=DCH),
                    wo_r[dt]), inc=[(s_wq[sl], 16)])
                if dt >= 2:
                    W("sync", s_ve, ve_x1[dt - 2])
                DO("sync", lambda e, dt=dt, sl=sl: e.dma_start(
                    xo_f[sl][:, :], x_own[dt * 128:(dt + 1) * 128, :]),
                   inc=[(s_xof[sl], 16)])
                W("tensor", s_wq[sl], C(s_wq[sl]))
                if dt >= 2:
                    W("tensor", s_se, se_wo[dt - 2])
                for hc in range(DCH):
                    for half in range(2):
                        DO("tensor", lambda e, hc=hc, half=half, sl=sl: e.matmul(
                            PS[:, sl * 1024 + half * 512:sl * 1024 + half * 512 + 512],
                            wo_bf[sl][:, hc * 128:(hc + 1) * 128],
                            aoT[:, hc * TOK + half * 512:hc * TOK + half * 512 + 512],
                            start=(hc == 0), stop=(hc == DCH - 1)))
                te_wo[dt] = temark()
                W("scalar", s_te, te_wo[dt])
                if dt >= 2:
                    W("scalar", s_ve, ve_x1[dt - 2])
                DO("scalar", lambda e, dt=dt, sl=sl: e.activation(
                    tmp_f[sl][:, :], PS[:, sl * 1024:sl * 1024 + 1024],
                    AF.Identity, bias=bo_sb[:, dt:dt + 1]), inc=[(s_se, 1)])
                se_wo[dt] = C(s_se)
                W("vector", s_se, se_wo[dt])
                W("vector", s_xof[sl], C(s_xof[sl]))
                DO("vector", lambda e, dt=dt, sl=sl: e.tensor_tensor(
                    x1[:, dt * TOK:(dt + 1) * TOK], tmp_f[sl][:, :],
                    xo_f[sl][:, :], ALU.add))
                DR("vector")
                if dt >= 2:
                    W("vector", s_te, te_st3[dt - 2])
                DO("vector", lambda e, dt=dt, sl=sl: e.tensor_tensor(
                    sq_t[sl][:, :], x1[:, dt * TOK:(dt + 1) * TOK],
                    x1[:, dt * TOK:(dt + 1) * TOK], ALU.mult))
                DR("vector")
                DO("vector", lambda e, dt=dt, sl=sl: e.tensor_copy(
                    xc_bf[sl][:, :], x1[:, dt * TOK:(dt + 1) * TOK]))
                ve_x1[dt] = vemark()

                def emit_st3(dq):
                    sq = dq % 2
                    W("tensor", s_ve, ve_x1[dq])
                    for half in range(2):
                        DO("tensor", lambda e, sq=sq, half=half, dq=dq: e.matmul(
                            PS[0:1, 2048 + half * 512:2048 + half * 512 + 512],
                            ones_bf[:, :],
                            xc_bf[sq][:, half * 512:half * 512 + 512],
                            start=(dq == 0), stop=(dq == DCH - 1)))
                    for half in range(2):
                        DO("tensor", lambda e, sq=sq, half=half, dq=dq: e.matmul(
                            PS[0:1, 3072 + half * 512:3072 + half * 512 + 512],
                            ones_bf[:, :], sq_t[sq][:, half * 512:half * 512 + 512],
                            start=(dq == 0), stop=(dq == DCH - 1)))
                    te_st3[dq] = temark()

                if dt >= 1:
                    emit_st3(dt - 1)
                if dt == DCH - 1:
                    emit_st3(dt)

            def ln_tail():
                W("vector", s_te, C(s_te))
                DO("vector", lambda e: e.tensor_scalar(
                    lt1[0:1, :], PS[0:1, 2048:3072], 1.0 / float(D),
                    None, ALU.mult))
                DO("vector", lambda e: e.tensor_scalar(
                    lt2[0:1, :], PS[0:1, 3072:4096], 1.0 / float(D),
                    None, ALU.mult))
                DR("vector")
                DO("vector", lambda e: e.tensor_tensor(
                    lt3[0:1, :], lt1[0:1, :], lt1[0:1, :],
                    ALU.mult))
                DR("vector")
                DO("vector", lambda e: e.tensor_tensor(
                    lt2[0:1, :], lt2[0:1, :], lt3[0:1, :],
                    ALU.subtract))
                vemark()
                W("scalar", s_ve, C(s_ve))
                DO("scalar", lambda e: e.activation(
                    lt3[0:1, :], lt2[0:1, :], AF.Ln,
                    bias=epsv[0:1, 0:1]))
                DR("scalar")
                DO("scalar", lambda e: e.activation(
                    lt2[0:1, :], lt3[0:1, :], AF.Exp, scale=-0.5),
                   inc=[(s_se, 1)])
                W("vector", s_se, C(s_se))
                DO("vector", lambda e: e.tensor_tensor(
                    lt1[0:1, :], lt1[0:1, :], lt2[0:1, :],
                    ALU.mult))
                DR("vector")
                DO("vector", lambda e: e.tensor_scalar(
                    lt1[0:1, :], lt1[0:1, :], -1.0, None, ALU.mult))
                vemark()
                W("tensor", s_ve, C(s_ve))
                for half in range(2):
                    DO("tensor", lambda e, half=half: e.matmul(
                        PS[:, half * 512:half * 512 + 512], ones1f[:, :],
                        lt2[0:1, half * 512:half * 512 + 512],
                        start=True, stop=True))
                for half in range(2):
                    DO("tensor", lambda e, half=half: e.matmul(
                        PS[:, 1024 + half * 512:1024 + half * 512 + 512],
                        ones1f[:, :], lt1[0:1, half * 512:half * 512 + 512],
                        start=True, stop=True))
                temark()
                W("vector", s_te, C(s_te))
                DO("vector", lambda e: e.tensor_copy(a_sb[:, :], PS[:, 0:1024]))
                DO("vector", lambda e: e.tensor_copy(b_sb[:, :], PS[:, 1024:2048]))
                vemark()

            ln_tail()
            ve_ln1 = {}
            for dt in range(DCH):
                sl = dt % 2
                DO("vector", lambda e, dt=dt, sl=sl: e.tensor_tensor(
                    tmp_f[sl][:, :], x1[:, dt * TOK:(dt + 1) * TOK], a_sb[:, :],
                    ALU.mult))
                DR("vector")
                DO("vector", lambda e, dt=dt, sl=sl: e.tensor_tensor(
                    x1[:, dt * TOK:(dt + 1) * TOK], tmp_f[sl][:, :], b_sb[:, :],
                    ALU.add))
                DR("vector")
                DO("vector", lambda e, dt=dt: e.tensor_copy(
                    x1b[:, dt * TOK:(dt + 1) * TOK],
                    x1[:, dt * TOK:(dt + 1) * TOK]))
                ve_ln1[dt] = vemark()
                W("tensor", s_ve, ve_ln1[dt])
                for half in range(2):
                    DO("tensor", lambda e, dt=dt, half=half: e.matmul(
                        PS[0:16, half * 512:half * 512 + 512],
                        gwt_sb[:, dt * 16:(dt + 1) * 16],
                        x1[:, dt * TOK + half * 512:dt * TOK + half * 512 + 512],
                        start=(dt == 0), stop=(dt == DCH - 1)))
            te_lg_pre = temark()

            if maybe_stop("ph3"):
                return
            barrier()
            state = {"sg_last": 0, "h_last": 0, "ce_last": 0, "y_last": {},
                     "slabfree": {}, "dslabfree": {}, "y_gate": 0,
                     "comb_last": 0}

            W("vector", s_te, te_lg_pre)
            DO("vector", lambda e: e.tensor_copy(lg_sb[:, :], PS[0:16, 0:1024]))
            ve_lg = vemark()
            ve_lgt = {}
            te_ftr = {}
            for tt in range(8):
                pb = 2 + tt % 2
                if tt == 0:
                    W("tensor", s_ve, ve_lg)
                if tt >= 2:
                    W("tensor", s_ve, ve_lgt[tt - 2])
                DO("tensor", lambda e, tt=tt, pb=pb: e.transpose(
                    PS[:, pb * 512:pb * 512 + 16],
                    lg_sb[0:16, tt * 128:(tt + 1) * 128], idf_sb[0:16, 0:16]))
                te_ftr[tt] = temark()
                W("vector", s_te, te_ftr[tt])
                DO("vector", lambda e, tt=tt, pb=pb: e.tensor_copy(
                    lgt8[:, tt * 16:(tt + 1) * 16], PS[:, pb * 512:pb * 512 + 16]))
                ve_lgt[tt] = vemark()
            state["h_last"] = ve_lgt[7]
            state["sg_last"] = C(s_se)

            ve_nm1 = {}
            ve_cmb = {}
            se_rexp = {}
            te_rtr = {}

            def hook_scalar(j):
                tt = j - 1
                if 0 <= tt < 8:
                    W("scalar", s_ve, ve_nm1[tt])
                    if tt >= 1:
                        W("scalar", s_ve, ve_cmb[tt - 1])
                    DO("scalar", lambda e, tt=tt: e.activation(
                        esc_t[:, :], lgt8[:, tt * 16:(tt + 1) * 16], AF.Exp,
                        bias=nm18[:, tt * 16:tt * 16 + 1]), inc=[(s_se, 1)])
                    se_rexp[tt] = C(s_se)

            def hook_vector(j):
                tt = j
                if tt < 8:
                    DO("vector", lambda e, tt=tt: e.reduce_max(
                        m1c[:, :], lgt8[:, tt * 16:(tt + 1) * 16], axis=AX.X))
                    DR("vector")
                    DO("vector", lambda e, tt=tt: e.tensor_scalar(
                        nm18[:, tt * 16:tt * 16 + 1], m1c[:, :], -1.0, None,
                        ALU.mult))
                    ve_nm1[tt] = vemark()
                tt = j - 1
                if 0 <= tt < 8:
                    W("vector", s_se, se_rexp[tt])
                    DO("vector", lambda e: e.reduce_sum(rsum_t[:, :], esc_t[:, :],
                                                        axis=AX.X))
                    DR("vector")
                    DO("vector", lambda e: e.reciprocal(rinv_t[:, :], rsum_t[:, :]))
                    DR("vector")
                    DO("vector", lambda e: e.tensor_scalar(
                        sc_t[:, :], esc_t[:, :], rinv_t[:, :], None, ALU.mult))
                    DR("vector")
                    DO("vector", lambda e: e.reduce_max(m1c[:, :], sc_t[:, :],
                                                        axis=AX.X))
                    DR("vector")
                    DO("vector", lambda e: e.tensor_scalar(
                        eqm_t[:, :], sc_t[:, :], m1c[:, :], None, ALU.is_equal))
                    DR("vector")
                    DO("vector", lambda e: e.tensor_scalar(
                        big_t[:, :], eqm_t[:, :], 1e30, None, ALU.mult))
                    DR("vector")
                    DO("vector", lambda e: e.tensor_tensor(
                        msk_t[:, :], sc_t[:, :], big_t[:, :], ALU.subtract))
                    DR("vector")
                    DO("vector", lambda e: e.reduce_max(m2c[:, :], msk_t[:, :],
                                                        axis=AX.X))
                    DR("vector")
                    DO("vector", lambda e: e.tensor_scalar(
                        keep_t[:, :], sc_t[:, :], m2c[:, :], None, ALU.is_ge))
                    DR("vector")
                    DO("vector", lambda e, tt=tt: e.tensor_tensor(
                        cmb8[:, tt * 16:(tt + 1) * 16], sc_t[:, :], keep_t[:, :],
                        ALU.mult))
                    ve_cmb[tt] = vemark()

            def hook_tensor_post(j):
                tt = j - 2
                if 0 <= tt < 8:
                    W("tensor", s_ve, ve_cmb[tt])
                    DO("tensor", lambda e, tt=tt: e.transpose(
                        PS[0:16, 2048:2176], cmb8[:, tt * 16:(tt + 1) * 16],
                        idf_sb[:, :]))
                    te_rtr[tt] = temark()

            def hook_vector_post(j):
                tt = j - 2
                if 0 <= tt < 8:
                    W("vector", s_te, te_rtr[tt])
                    DO("vector", lambda e, tt=tt: e.tensor_copy(
                        comb_sb[0:16, tt * 128:(tt + 1) * 128],
                        PS[0:16, 2048:2176]))
                    state["comb_last"] = vemark()
                    state["y_gate"] = state["comb_last"]

            def gu_block(tag, slab_src, use_ce, hooks=False):
                ve_h_tiles = {}
                for j in range(NJ):
                    sl = j % 2
                    key = (tag, j - 2)
                    if key in state["slabfree"]:
                        W("gpsimd", s_te, state["slabfree"][key])
                    DO("gpsimd", lambda e, j=j, sl=sl: e.dma_start(
                        wgu_bf[sl][:, :].rearrange("p (u f) -> p u f", u=2),
                        slab_src(j)), inc=[(s_wg[sl], 16)])
                    W("tensor", s_wg[sl], C(s_wg[sl]))
                    W("tensor", s_se, state["sg_last"])
                    for hc in range(DCH):
                        for half in range(2):
                            DO("tensor", lambda e, hc=hc, half=half, sl=sl: e.matmul(
                                PS[:, half * 512:half * 512 + 512],
                                wgu_bf[sl][:, hc * 128:(hc + 1) * 128],
                                x1b[:, hc * TOK + half * 512:hc * TOK + half * 512 + 512],
                                start=(hc == 0), stop=(hc == DCH - 1)))
                    te_g = temark()
                    W("tensor", s_ve, state["h_last"])
                    if use_ce:
                        W("tensor", s_ve, state["ce_last"])
                    for hc in range(DCH):
                        for half in range(2):
                            DO("tensor", lambda e, hc=hc, half=half, sl=sl: e.matmul(
                                PS[:, 1024 + half * 512:1024 + half * 512 + 512],
                                wgu_bf[sl][:, D + hc * 128:D + hc * 128 + 128],
                                x1b[:, hc * TOK + half * 512:hc * TOK + half * 512 + 512],
                                start=(hc == 0), stop=(hc == DCH - 1)))
                    te_u = temark()
                    state["slabfree"][(tag, j)] = te_u
                    if hooks:
                        hook_scalar(j)
                    W("scalar", s_te, te_g)
                    if j >= 2:
                        W("scalar", s_ve, ve_h_tiles[j - 2])
                    DO("scalar", lambda e, sl=sl: e.activation(
                        sg_bf[sl][:, :], PS[:, 0:1024], AF.Silu),
                       inc=[(s_se, 1)])
                    state["sg_last"] = C(s_se)
                    W("vector", s_te, te_u)
                    if use_ce:
                        DO("vector", lambda e, sl=sl: e.tensor_tensor(
                            uf_sb[sl][:, :], PS[:, 1024:2048], ce_sb[:, :],
                            ALU.mult))
                    else:
                        DO("vector", lambda e, sl=sl: e.tensor_copy(
                            uf_sb[sl][:, :], PS[:, 1024:2048]))
                    DR("vector")
                    W("vector", s_se, state["sg_last"])
                    DO("vector", lambda e, j=j, sl=sl: e.tensor_tensor(
                        hpr[:, j * TOK:(j + 1) * TOK], sg_bf[sl][:, :],
                        uf_sb[sl][:, :], ALU.mult))
                    ve_h_tiles[j] = vemark()
                    state["h_last"] = ve_h_tiles[j]
                    if hooks:
                        hook_vector(j)
                        hook_tensor_post(j)
                        hook_vector_post(j)

            def down_block(tag, slab_src):
                h_ready = state["h_last"]
                for hp in range(8):
                    sl = hp % 2
                    key = (tag, hp - 2)
                    if key in state["dslabfree"]:
                        W("gpsimd", s_te, state["dslabfree"][key])
                    DO("gpsimd", lambda e, hp=hp, sl=sl: e.dma_start(
                        wd_bf[sl][:, :].rearrange("p (u f) -> p u f", u=2),
                        slab_src(hp)), inc=[(s_wd[sl], 16)])
                    W("tensor", s_wd[sl], C(s_wd[sl]))
                    if hp == 0:
                        W("tensor", s_ve, h_ready)
                        W("tensor", s_ve, state["y_gate"])
                    for tp in range(2):
                        ht = hp * 2 + tp
                        yb = 4 + (ht % 2) * 2
                        if ht >= 2:
                            W("tensor", s_ve, state["y_last"][(tag, ht - 2)])
                        for ic in range(NJ):
                            for half in range(2):
                                DO("tensor", lambda e, tp=tp, ic=ic, half=half, sl=sl, yb=yb: e.matmul(
                                    PS[:, yb * 512 + half * 512:yb * 512 + half * 512 + 512],
                                    wd_bf[sl][:, tp * NJ * 128 + ic * 128:tp * NJ * 128 + ic * 128 + 128],
                                    hpr[:, ic * TOK + half * 512:ic * TOK + half * 512 + 512],
                                    start=(ic == 0), stop=(ic == NJ - 1)))
                        te_y = temark()
                        W("vector", s_te, te_y)
                        DO("vector", lambda e, ht=ht, yb=yb: e.tensor_tensor(
                            x1[:, ht * TOK:(ht + 1) * TOK],
                            PS[:, yb * 512:yb * 512 + 1024],
                            x1[:, ht * TOK:(ht + 1) * TOK], ALU.add))
                        state["y_last"][(tag, ht)] = vemark()
                    state["dslabfree"][(tag, hp)] = C(s_te)

            for r in range(2):
                tag = "sh%d" % r
                gu_block(tag,
                         lambda j, r=r: sgu_r[r * NJ + j].rearrange(
                             "u p h c -> p u (h c)"),
                         use_ce=False, hooks=(r == 0))
                down_block(tag,
                           lambda hp, r=r: sd_r[hp][:, :, r * NJ:(r + 1) * NJ, :]
                           .rearrange("u p i c -> p u (i c)"))

            for ex in range(NE):
                tag = "e%d" % ex
                W("tensor", s_ve, state["h_last"])
                W("tensor", s_ve, state["comb_last"])
                for half in range(2):
                    DO("tensor", lambda e, ex=ex, half=half: e.matmul(
                        PS[:, 1024 + half * 512:1024 + half * 512 + 512],
                        sel_sb[:, ex * 128:(ex + 1) * 128],
                        comb_sb[0:16, half * 512:half * 512 + 512],
                        start=True, stop=True))
                te_ce = temark()
                W("vector", s_te, te_ce)
                DO("vector", lambda e: e.tensor_copy(ce_sb[:, :], PS[:, 1024:2048]))
                state["ce_last"] = vemark()
                gu_block(tag,
                         lambda j, ex=ex: wgu_r[ex, j].rearrange(
                             "u p h c -> p u (h c)"),
                         use_ce=True)
                down_block(tag,
                           lambda hp, ex=ex: wd_r[ex, hp].rearrange(
                               "u p i c -> p u (i c)"))

            if maybe_stop("ph6"):
                return
            barrier()
            te_st7 = {}
            for dt in range(DCH):
                sl = dt % 2
                if dt >= 2:
                    W("vector", s_te, te_st7[dt - 2])
                DO("vector", lambda e, dt=dt, sl=sl: e.tensor_tensor(
                    tmp7[sl][:, :], x1[:, dt * TOK:(dt + 1) * TOK],
                    x1[:, dt * TOK:(dt + 1) * TOK], ALU.mult))
                ve_sq7 = vemark()
                W("tensor", s_ve, ve_sq7)
                for half in range(2):
                    DO("tensor", lambda e, dt=dt, half=half: e.matmul(
                        PS[0:1, 2048 + half * 512:2048 + half * 512 + 512],
                        onesf[:, :],
                        x1[:, dt * TOK + half * 512:dt * TOK + half * 512 + 512],
                        start=(dt == 0), stop=(dt == DCH - 1)))
                for half in range(2):
                    DO("tensor", lambda e, sl=sl, half=half, dt=dt: e.matmul(
                        PS[0:1, 3072 + half * 512:3072 + half * 512 + 512],
                        onesf[:, :], tmp7[sl][:, half * 512:half * 512 + 512],
                        start=(dt == 0), stop=(dt == DCH - 1)))
                te_st7[dt] = temark()
            ln_tail()
            for dt in range(DCH):
                sl = dt % 2
                DO("vector", lambda e, dt=dt, sl=sl: e.tensor_tensor(
                    tmp7[sl][:, :], x1[:, dt * TOK:(dt + 1) * TOK], a_sb[:, :],
                    ALU.mult))
                DR("vector")
                DO("vector", lambda e, dt=dt, sl=sl: e.tensor_tensor(
                    x1[:, dt * TOK:(dt + 1) * TOK], tmp7[sl][:, :], b_sb[:, :],
                    ALU.add))
                vemark()
            W("sync", s_ve, C(s_ve))
            DO("sync", lambda e: e.dma_start(
                out.rearrange("(dc p) t -> p dc t", p=128),
                x1[:, :].rearrange("p (dc t) -> p dc t", dc=DCH)),
               inc=[(s_st, 16)])
            W("sync", s_st, C(s_st))

        for name in ENGINES:
            def body(eng, name=name):
                pp.reset(name, eng)
                emit(pp)
            getattr(blk, name)(body)

    return nc


def _prep(inputs):
    f32 = np.float32
    c = np.ascontiguousarray
    x = np.asarray(inputs["L0_attn_x"], f32)
    wq = np.asarray(inputs["L0_attn_wq"], f32)
    bq = np.asarray(inputs["L0_attn_bq"], f32)
    wk = np.asarray(inputs["L0_attn_wk"], f32)
    bk = np.asarray(inputs["L0_attn_bk"], f32)
    wv = np.asarray(inputs["L0_attn_wv"], f32)
    bv = np.asarray(inputs["L0_attn_bv"], f32)
    wo = np.asarray(inputs["L0_attn_wo"], f32)
    bo = np.asarray(inputs["L0_attn_bo"], f32)
    gate_w = np.asarray(inputs["L0_ffn_gate_w"], f32)
    wgu = np.asarray(inputs["L0_ffn_w_gate_up"], f32)
    wd = np.asarray(inputs["L0_ffn_w_down"], f32)
    sgu = np.asarray(inputs["L0_ffn_shared_gate_up"], f32)
    sd = np.asarray(inputs["L0_ffn_shared_down"], f32)

    xT = c(x.transpose(0, 2, 1))
    wq_r = c(wq.reshape(DCH, 128, NH, 128).transpose(2, 1, 0, 3))
    wk_r = c(wk.reshape(DCH, 128, G, 128).transpose(2, 1, 0, 3))
    wv_r = c(wv.reshape(DCH, 128, G, 128).transpose(2, 1, 0, 3))
    wo_r = c(wo.reshape(DCH, 128, DCH, 128).transpose(2, 1, 0, 3))
    bq_r = c(bq.reshape(NH, 128).T)
    bk_r = c(bk.reshape(G, 128).T)
    bv_row = c(bv.reshape(1, G * 128))
    bo_r = c(bo.reshape(DCH, 128).T)
    gwt = c(gate_w.T)
    wg5 = wgu.reshape(NE, DCH, 128, 2 * NJ, 128).transpose(0, 3, 2, 1, 4)
    wgu_r = c(np.stack([wg5[:, :NJ], wg5[:, NJ:]], axis=2))
    wd5 = wd.reshape(NE, NJ, 128, DCH, 128).transpose(0, 3, 2, 1, 4)
    wd_r = c(wd5.reshape(NE, 8, 2, 128, NJ, 128))
    sg4 = sgu.reshape(DCH, 128, 2 * SJ, 128).transpose(2, 1, 0, 3)
    sgu_r = c(np.stack([sg4[:SJ], sg4[SJ:]], axis=1))
    sd4 = sd.reshape(SJ, 128, DCH, 128).transpose(2, 1, 0, 3)
    sd_r = c(sd4.reshape(8, 2, 128, SJ, 128))

    pos = np.arange(S, dtype=np.float64)
    invf = 1.0 / (10000.0 ** (np.arange(0, HD, 2, dtype=np.float64) / HD))
    fr = np.outer(pos, invf)
    cos_t = np.cos(fr).T
    sin_t = np.sin(fr).T
    bf16 = ml_dtypes.bfloat16
    cosk = np.concatenate([cos_t, cos_t], axis=0).astype(bf16)
    sink = np.concatenate([-sin_t, sin_t], axis=0).astype(bf16)
    idf = np.eye(128, dtype=f32)
    selm = np.zeros((NE, NE * 128), f32)
    for e in range(NE):
        selm[e, e * 128:(e + 1) * 128] = 1.0

    shared = dict(wq_r=wq_r, wk_r=wk_r, wv_r=wv_r, wo_r=wo_r, bq_r=bq_r,
                  bk_r=bk_r, bv_row=bv_row, bo_r=bo_r, gwt=gwt, wgu_r=wgu_r,
                  wd_r=wd_r, sgu_r=sgu_r, sd_r=sd_r, cosk=cosk, sink=sink,
                  idf=idf, sel=selm)
    in_maps = []
    for core in range(8):
        b, qc = core // 4, core % 4
        m = dict(shared)
        m["x_batch"] = xT[b]
        m["x_own"] = c(xT[b][:, qc * TOK:(qc + 1) * TOK])
        m["cosq"] = c(cosk[:, qc * TOK:(qc + 1) * TOK])
        m["sinq"] = c(sink[:, qc * TOK:(qc + 1) * TOK])
        in_maps.append(m)
    return in_maps


def kernel(**inputs):
    in_maps = _prep(inputs)
    nc = build_program()
    res = run_bass_kernel_spmd(nc, in_maps, core_ids=list(range(8)))
    out = np.empty((2, S, D), np.float32)
    for core in range(8):
        b, qc = core // 4, core % 4
        out[b, qc * TOK:(qc + 1) * TOK, :] = res.results[core]["out"].T
    return out


# version 11
# speedup vs baseline: 1.0460x; 1.0460x over previous
import numpy as np
import ml_dtypes
from contextlib import ExitStack

import concourse.bass as bass
import concourse.mybir as mybir
from concourse.bass_utils import run_bass_kernel_spmd

F32 = mybir.dt.float32
BF16 = mybir.dt.bfloat16
AF = mybir.ActivationFunctionType
ALU = mybir.AluOpType
AX = mybir.AxisListType

D = 2048
S = 4096
TOK = 1024
NH = 16
G = 2
HD = 128
DCH = 16
NKT = 32
NE = 16
NJ = 11
SJ = 22
ENGINES = ("sync", "gpsimd", "tensor", "vector", "scalar")
KB = 1024


class P:

    def __init__(self):
        self.counts = {}
        self.active = None
        self.eng = None
        self.sems = []

    def sem(self, handle):
        self.counts[id(handle)] = 0
        self.sems.append(handle)
        return handle

    def reset(self, name, eng):
        self.active = name
        self.eng = eng
        for k in self.counts:
            self.counts[k] = 0

    def do(self, engine, thunk, inc=None):
        if self.active == engine:
            inst = thunk(self.eng)
            self.last_inst = inst
            if inc is not None:
                for s, n in inc:
                    inst.then_inc(s, n)
        if inc is not None:
            for s, n in inc:
                self.counts[id(s)] += n

    def mark(self, engine, sem, n=1):
        if self.active == engine:
            self.last_inst.then_inc(sem, n)
        self.counts[id(sem)] += n

    def wait(self, engine, sem, val=None):
        tgt = self.counts[id(sem)] if val is None else val
        if tgt <= 0:
            return
        if self.active == engine:
            self.eng.wait_ge(sem, tgt)

    def count(self, sem):
        return self.counts[id(sem)]

    def drain(self, engine):
        if self.active == engine:
            self.eng.drain()


def build_program(stop_after=None):
    nc = bass.Bass("TRN2", target_bir_lowering=False, detect_race_conditions=False)

    def din(name, shape, dt=F32):
        return nc.dram_tensor(name, list(shape), dt, kind="ExternalInput").ap()

    x_batch = din("x_batch", [D, S])
    x_own = din("x_own", [D, TOK])
    wq_r = din("wq_r", [NH, 128, DCH, 128])
    wk_r = din("wk_r", [G, 128, DCH, 128])
    wv_r = din("wv_r", [G, 128, DCH, 128])
    wo_r = din("wo_r", [DCH, 128, DCH, 128])
    bq_r = din("bq_r", [128, NH])
    bk_r = din("bk_r", [128, G])
    bv_row = din("bv_row", [1, G * 128])
    bo_r = din("bo_r", [128, DCH])
    gwt = din("gwt", [D, NE])
    wgu_r = din("wgu_r", [NE, NJ, 2, 128, DCH, 128])
    wd_r = din("wd_r", [NE, 8, 2, 128, NJ, 128])
    sgu_r = din("sgu_r", [SJ, 2, 128, DCH, 128])
    sd_r = din("sd_r", [8, 2, 128, SJ, 128])
    cosq = din("cosq", [128, TOK], BF16)
    sinq = din("sinq", [128, TOK], BF16)
    cosk = din("cosk", [128, S], BF16)
    sink = din("sink", [128, S], BF16)
    idf = din("idf", [128, 128])
    sel = din("sel", [NE, NE * 128])

    out = nc.dram_tensor("out", [D, TOK], F32, kind="ExternalOutput").ap()

    ARENA = 202 * KB
    arena = nc.alloc_sbuf_tensor("arena", [128, ARENA], mybir.dt.uint8)
    base = nc.lookup_mloc(arena).addr

    def at(name, shape, dt, off):
        assert off % 32 == 0, name
        need = int(np.prod(shape[1:])) * mybir.dt.size(dt)
        assert off + need <= ARENA, f"{name}: {off}+{need} > {ARENA}"
        return nc.alloc_sbuf_tensor_at(name, list(shape), dt, offset=base + off).ap()

    ones_bf = at("ones_bf", [128, 1], BF16, 0)
    ones1f = at("ones1f", [1, 128], F32, 32)
    ones1b = at("ones1b", [1, 128], BF16, 576)
    onesf = at("onesf", [128, 1], F32, 864)
    epsv = at("epsv", [1, 1], F32, 896)
    bq_sb = at("bq_sb", [128, NH], F32, 928)
    bk_sb = at("bk_sb", [128, G], F32, 992)
    bv_bf = at("bv_bf", [1, G * 128], BF16, 1024)
    bo_sb = at("bo_sb", [128, DCH], F32, 1568)
    qmax = at("qmax", [1, NH], F32, 1632)
    kmax = at("kmax", [1, G], F32, 1696)
    negc = at("negc", [128, NH], F32, 1760)
    gwt_sb = at("gwt_sb", [128, DCH * NE], F32, 1824)
    idf_sb = at("idf_sb", [128, 128], F32, 2848)
    comb_sb = at("comb_sb", [16, 8 * 128], F32, 3392)
    sel_sb = at("sel_sb", [NE, NE * 128], F32, 7488)
    sc1 = at("sc1", [1, 1], F32, 15680)
    qt = at("qt", [128, NH * TOK], BF16, 16 * KB)
    x1b = at("x1b", [128, DCH * TOK], BF16, 16 * KB)
    xb_bf = [at("xb_bf0", [128, DCH * 512], BF16, 48 * KB),
             at("xb_bf1", [128, DCH * 512], BF16, 64 * KB)]
    ktb16 = at("ktb16", [128, G * S], BF16, 144 * KB)
    cosk_sb = at("cosk_sb", [128, S], BF16, 168 * KB)
    sink_sb = at("sink_sb", [128, S], BF16, 176 * KB)
    x_own_bf = at("x_own_bf", [128, DCH * TOK], BF16, 48 * KB)
    wq_bf = [at("wq_bf0", [128, D], BF16, 80 * KB),
             at("wq_bf1", [128, D], BF16, 84 * KB)]
    qf = [at("qf0", [128, TOK], BF16, 88 * KB),
          at("qf1", [128, TOK], BF16, 90 * KB)]
    qrot = [at("qrot0", [128, TOK], BF16, 92 * KB),
            at("qrot1", [128, TOK], BF16, 94 * KB)]
    krot = [at("krot0", [128, TOK], BF16, 160 * KB),
            at("krot1", [128, TOK], BF16, 162 * KB)]
    t1_t = at("t1_t", [128, TOK], BF16, 104 * KB)
    t2_t = at("t2_t", [128, TOK], BF16, 106 * KB)
    nct = at("nct", [1, 64], F32, 108 * KB)
    et = [at("et%d" % i, [128, TOK], BF16, (88 + 2 * i) * KB)
          for i in range(12)]
    invsb = at("invsb", [128, TOK], F32, 152 * KB)
    tb1 = at("tb1", [128, TOK], BF16, 156 * KB)
    tb2 = at("tb2", [128, TOK], BF16, 158 * KB)
    tb3 = at("tb3", [128, TOK], BF16, 200 * KB)
    rspart = at("rspart", [128, TOK], F32, 196 * KB)
    x1 = at("x1", [128, DCH * TOK], F32, 48 * KB)
    vn = at("vn", [128, G * S], BF16, 112 * KB)
    ktbf = at("ktbf", [128, G * S], BF16, 128 * KB)
    wo_bf = [at("wo_bf0", [128, D], BF16, 112 * KB),
             at("wo_bf1", [128, D], BF16, 116 * KB)]
    tmp_f = [at("tmp_f0", [128, TOK], F32, 120 * KB),
             at("tmp_f1", [128, TOK], F32, 124 * KB)]
    xo_f = [at("xo_f0", [128, TOK], F32, 128 * KB),
            at("xo_f1", [128, TOK], F32, 132 * KB)]
    sq_t = [at("sq_t0", [128, TOK], BF16, 136 * KB),
            at("sq_t1", [128, TOK], BF16, 138 * KB)]
    xc_bf = [at("xc_bf0", [128, TOK], BF16, 140 * KB),
             at("xc_bf1", [128, TOK], BF16, 142 * KB)]
    lg_sb = at("lg_sb", [16, TOK], F32, 144 * KB)
    lgt_t = at("lgt_t", [128, 16], F32, 148 * KB)
    lgt8 = at("lgt8", [128, 8 * 16], F32, 148 * KB + 512)
    nm18 = at("nm18", [128, 8 * 16], F32, 149 * KB)
    cmb8 = at("cmb8", [128, 8 * 16], F32, 149 * KB + 512)
    m1c = at("m1c", [128, 1], F32, 150 * KB)
    nm1 = at("nm1", [128, 1], F32, 150 * KB + 64)
    rsum_t = at("rsum_t", [128, 1], F32, 150 * KB + 128)
    rinv_t = at("rinv_t", [128, 1], F32, 150 * KB + 192)
    m2c = at("m2c", [128, 1], F32, 150 * KB + 256)
    esc_t = at("esc_t", [128, 16], F32, 150 * KB + 320)
    sc_t = at("sc_t", [128, 16], F32, 150 * KB + 384)
    eqm_t = at("eqm_t", [128, 16], F32, 150 * KB + 448)
    big_t = at("big_t", [128, 16], F32, 150 * KB + 512)
    msk_t = at("msk_t", [128, 16], F32, 150 * KB + 576)
    keep_t = at("keep_t", [128, 16], F32, 150 * KB + 640)
    cmb_t = at("cmb_t", [128, 16], F32, 150 * KB + 704)
    wgu_bf = [at("wgu_bf0", [128, 2 * D], BF16, 112 * KB),
              at("wgu_bf1", [128, 2 * D], BF16, 120 * KB)]
    wd_bf = [at("wd_bf0", [128, 2 * NJ * 128], BF16, 128 * KB),
             at("wd_bf1", [128, 2 * NJ * 128], BF16, 134 * KB)]
    tmp7 = [at("tmp7_0", [128, TOK], F32, 112 * KB),
            at("tmp7_1", [128, TOK], F32, 116 * KB)]
    wk_bf = at("wk_bf", [128, G * D], BF16, 184 * KB)
    wv_bf = at("wv_bf", [128, G * D], BF16, 192 * KB)
    sqk_bf = [at("sqk_bf0", [128, TOK], BF16, 164 * KB),
              at("sqk_bf1", [128, TOK], BF16, 166 * KB)]
    sq_bf = [at("sq_bf0", [128, TOK], BF16, 184 * KB),
             at("sq_bf1", [128, TOK], BF16, 186 * KB),
             at("sq_bf2", [128, TOK], BF16, 188 * KB),
             at("sq_bf3", [128, TOK], BF16, 194 * KB)]
    cosq_sb = at("cosq_sb", [128, TOK], BF16, 190 * KB)
    sinq_sb = at("sinq_sb", [128, TOK], BF16, 192 * KB)
    aoT = at("aoT", [128, NH * TOK], BF16, 160 * KB)
    a_sb = at("a_sb", [128, TOK], F32, 144 * KB)
    b_sb = at("b_sb", [128, TOK], F32, 148 * KB)
    lt1 = at("lt1b", [1, TOK], F32, 152 * KB)
    lt2 = at("lt2b", [1, TOK], F32, 156 * KB)
    lt3 = at("lt3b", [1, TOK], F32, 192 * KB)
    hpr = at("hpr", [128, NJ * TOK], BF16, 160 * KB)
    sg_bf = [at("sg_bf0", [128, TOK], BF16, 182 * KB),
             at("sg_bf1", [128, TOK], BF16, 184 * KB)]
    uf_sb = [at("uf_sb0", [128, TOK], F32, 186 * KB),
             at("uf_sb1", [128, TOK], F32, 190 * KB)]
    ce_sb = at("ce_sb", [128, TOK], F32, 194 * KB)

    es = ExitStack()
    with es:
        PS = es.enter_context(nc.psum_tensor("PS", [128, 4096], F32))

        sems = {}

        def mksem(name):
            sems[name] = es.enter_context(nc.semaphore(name))
            return sems[name]

        s_te = mksem("s_te")
        s_ve = mksem("s_ve")
        s_se = mksem("s_se")
        s_st = mksem("s_st")
        s_const = mksem("s_const")
        s_kv = mksem("s_kv")
        s_xb = [mksem("s_xb0"), mksem("s_xb1")]
        s_qr = [mksem("s_qr0"), mksem("s_qr1")]
        s_kr = mksem("s_kr")
        s_xobf = mksem("s_xobf")
        s_wq = [mksem("s_wq0"), mksem("s_wq1")]
        s_xof = [mksem("s_xof0"), mksem("s_xof1")]
        s_wg = [mksem("s_wg0"), mksem("s_wg1")]
        s_wd = [mksem("s_wd0"), mksem("s_wd1")]

        blk = es.enter_context(nc.Block())
        pp = P()
        for s in sems.values():
            pp.sem(s)

        def emit(p):
            W, C, DO, DR = p.wait, p.count, p.do, p.drain

            def bank(b, w=512):
                return PS[:, b * 512:b * 512 + w]

            def temark(eng="tensor"):
                p.mark(eng, s_te)
                return C(s_te)

            def vemark():
                DR("vector")
                DO("vector", lambda e: e.engine_nop(), inc=[(s_ve, 1)])
                return C(s_ve)

            def barrier():
                for ename in ENGINES:
                    for s in pp.sems:
                        if s is not s_st:
                            W(ename, s)

            def maybe_stop(name):
                if stop_after != name:
                    return False
                barrier()
                W("sync", s_ve, C(s_ve))
                DO("sync", lambda e: e.dma_start(
                    out.rearrange("(dc p) t -> p dc t", p=128),
                    x1[:, :].rearrange("p (dc t) -> p dc t", dc=DCH)),
                   inc=[(s_st, 16)])
                W("sync", s_st, C(s_st))
                return True

            for dst, src in ((bq_sb, bq_r), (bk_sb, bk_r), (bo_sb, bo_r),
                             (idf_sb, idf), (sel_sb, sel),
                             (cosk_sb, cosk), (sink_sb, sink)):
                DO("sync", lambda e, d=dst, s=src: e.dma_start(d[:, :], s),
                   inc=[(s_const, 16)])
            DO("sync", lambda e: e.dma_start(
                gwt_sb[:, :].rearrange("p (h e2) -> p h e2", h=DCH),
                gwt.rearrange("(h p) e2 -> p h e2", p=128)), inc=[(s_const, 16)])
            const_tot = C(s_const)
            DO("gpsimd", lambda e: e.dma_start(bv_bf[:, :], bv_row), inc=[(s_kv, 16)])
            DO("gpsimd", lambda e: e.dma_start(
                wk_bf[:, :].rearrange("p (g f) -> p g f", g=G),
                wk_r.rearrange("g p h c -> p g (h c)")), inc=[(s_kv, 16)])
            DO("gpsimd", lambda e: e.dma_start(
                wv_bf[:, :].rearrange("p (g f) -> p g f", g=G),
                wv_r.rearrange("g p h c -> p g (h c)")), inc=[(s_kv, 16)])
            kv_tot = C(s_kv)
            for buf, val in ((ones_bf, 1.0), (ones1f, 1.0), (ones1b, 1.0),
                             (onesf, 1.0), (epsv, 1e-5), (qmax, 0.0),
                             (kmax, 0.0)):
                DO("vector", lambda e, b=buf, v=val: e.memset(b[:, :], v))
            ve_init = vemark()

            if maybe_stop("ph0"):
                return
            te_k, te_v, te_xbfree, se_kvev = {}, {}, {}, {}
            for ch in range(8):
                sl = ch % 2
                if ch >= 2:
                    W("gpsimd", s_te, te_xbfree[ch - 2])
                DO("gpsimd", lambda e, ch=ch, sl=sl: e.dma_start(
                    xb_bf[sl][:, :].rearrange("p (h t) -> p h t", h=DCH),
                    x_batch[:, ch * 512:(ch + 1) * 512]
                    .rearrange("(h p) t -> p h t", p=128)), inc=[(s_xb[sl], 16)])
                W("tensor", s_xb[sl], C(s_xb[sl]))
                if ch == 0:
                    W("tensor", s_kv, kv_tot)
                    W("tensor", s_ve, ve_init)
                for g in range(G):
                    kb, vb = g * 2 + sl, 4 + g * 2 + sl
                    if ch >= 2:
                        W("tensor", s_se, se_kvev[(ch - 2, g, 0)])
                    for hc in range(DCH):
                        DO("tensor", lambda e, g=g, hc=hc, sl=sl, kb=kb: e.matmul(
                            bank(kb),
                            wk_bf[:, g * D + hc * 128:g * D + hc * 128 + 128],
                            xb_bf[sl][:, hc * 512:(hc + 1) * 512],
                            start=(hc == 0), stop=(hc == DCH - 1)))
                    te_k[(ch, g)] = temark()
                    if ch >= 2:
                        W("tensor", s_se, se_kvev[(ch - 2, g, 1)])
                    for tt in range(4):
                        for hc in range(DCH):
                            DO("tensor", lambda e, g=g, hc=hc, tt=tt, sl=sl, vb=vb: e.matmul(
                                PS[:, vb * 512 + tt * 128:vb * 512 + tt * 128 + 128],
                                xb_bf[sl][:, hc * 512 + tt * 128:hc * 512 + tt * 128 + 128],
                                wv_bf[:, g * D + hc * 128:g * D + hc * 128 + 128],
                                start=(hc == 0), stop=False))
                        DO("tensor", lambda e, g=g, tt=tt, vb=vb: e.matmul(
                            PS[:, vb * 512 + tt * 128:vb * 512 + tt * 128 + 128],
                            ones1b[:, :], bv_bf[0:1, g * 128:(g + 1) * 128],
                            start=False, stop=True))
                    te_v[(ch, g)] = temark()
                te_xbfree[ch] = C(s_te)
                for g in range(G):
                    kb, vb = g * 2 + sl, 4 + g * 2 + sl
                    W("scalar", s_te, te_k[(ch, g)])
                    DO("scalar", lambda e, g=g, ch=ch, kb=kb: e.activation(
                        ktb16[:, g * S + ch * 512:g * S + (ch + 1) * 512],
                        bank(kb), AF.Identity, bias=bk_sb[:, g:g + 1]),
                       inc=[(s_se, 1)])
                    se_kvev[(ch, g, 0)] = C(s_se)
                    W("scalar", s_te, te_v[(ch, g)])
                    DO("scalar", lambda e, g=g, ch=ch, vb=vb: e.copy(
                        vn[:, (g * NKT + ch * 4) * 128:(g * NKT + ch * 4) * 128 + 512],
                        bank(vb)), inc=[(s_se, 1)])
                    se_kvev[(ch, g, 1)] = C(s_se)

            if maybe_stop("ph1b"):
                return
            W("gpsimd", s_te, te_xbfree[6])
            DO("gpsimd", lambda e: e.dma_start(
                x_own_bf[:, 0:8 * TOK].rearrange("p (h t) -> p h t", h=8),
                x_own[0:1024, :].rearrange("(h p) t -> p h t", p=128)),
               inc=[(s_xobf, 16)])
            W("gpsimd", s_te, te_xbfree[7])
            DO("gpsimd", lambda e: e.dma_start(
                x_own_bf[:, 8 * TOK:16 * TOK].rearrange("p (h t) -> p h t", h=8),
                x_own[1024:2048, :].rearrange("(h p) t -> p h t", p=128)),
               inc=[(s_xobf, 16)])
            xobf_tot = C(s_xobf)
            barrier()
            DO("sync", lambda e: e.dma_start(cosq_sb[:, :], cosq),
               inc=[(s_const, 16)])
            DO("sync", lambda e: e.dma_start(sinq_sb[:, :], sinq),
               inc=[(s_const, 16)])
            te_q, se_q, ve_q, ve_qn, te_qn_map = {}, {}, {}, {}, {}
            ve_kn, te_kn_map, ve_krope = {}, {}, {}
            se_kv_last = C(s_se)

            def emit_qn_pe(hq):
                W("tensor", s_ve, ve_q[hq])
                if hq >= 1:
                    W("tensor", s_ve, ve_qn[hq - 1])
                for half in range(2):
                    DO("tensor", lambda e, half=half, hq=hq: e.matmul(
                        PS[0:1, 2048 + half * 512:2048 + half * 512 + 512],
                        ones_bf[:, :],
                        sq_bf[hq % 4][:, half * 512:half * 512 + 512],
                        start=True, stop=True))
                te_qn_map[hq] = temark()

            def emit_qn_dve(hq):
                W("vector", s_te, te_qn_map[hq])
                DO("vector", lambda e: e.reduce_max(
                    sc1[0:1, 0:1], PS[0:1, 2048:3072], axis=AX.X))
                DR("vector")
                DO("vector", lambda e, hq=hq: e.tensor_copy(
                    qmax[0:1, hq:hq + 1], sc1[0:1, 0:1]))
                ve_qn[hq] = vemark()

            def emit_kn_pe(s):
                W("tensor", s_ve, ve_krope[s])
                if s >= 1:
                    W("tensor", s_ve, ve_kn[s - 1])
                for half in range(2):
                    DO("tensor", lambda e, half=half, s=s: e.matmul(
                        PS[0:1, 3072 + half * 512:3072 + half * 512 + 512],
                        ones_bf[:, :],
                        sqk_bf[s % 2][:, half * 512:half * 512 + 512],
                        start=True, stop=True))
                te_kn_map[s] = temark()

            def emit_kn_dve(s):
                W("vector", s_te, te_kn_map[s])
                DO("vector", lambda e: e.reduce_max(
                    sc1[0:1, 0:1], PS[0:1, 3072:4096], axis=AX.X))
                DR("vector")
                DO("vector", lambda e, s=s: e.tensor_tensor(
                    kmax[0:1, s // 4:s // 4 + 1], kmax[0:1, s // 4:s // 4 + 1],
                    sc1[0:1, 0:1], ALU.max))
                ve_kn[s] = vemark()

            for h in range(NH):
                sl = h % 2
                ks = h
                if h >= 2:
                    W("gpsimd", s_te, te_q[h - 2])
                DO("gpsimd", lambda e, h=h, sl=sl: e.dma_start(
                    wq_bf[sl][:, :].rearrange("p (hc c) -> p hc c", hc=DCH),
                    wq_r[h]), inc=[(s_wq[sl], 16)])
                W("tensor", s_wq[sl], C(s_wq[sl]))
                if h == 0:
                    W("tensor", s_xobf, xobf_tot)
                if h >= 2:
                    W("tensor", s_se, se_q[h - 2])
                for hc in range(DCH):
                    for half in range(2):
                        DO("tensor", lambda e, hc=hc, half=half, sl=sl: e.matmul(
                            PS[:, sl * 1024 + half * 512:sl * 1024 + half * 512 + 512],
                            wq_bf[sl][:, hc * 128:(hc + 1) * 128],
                            x_own_bf[:, hc * 1024 + half * 512:hc * 1024 + half * 512 + 512],
                            start=(hc == 0), stop=(hc == DCH - 1)))
                te_q[h] = temark()
                if h >= 3:
                    emit_qn_pe(h - 3)
                if h >= 1 and (h - 1) < 8:
                    emit_kn_pe(h - 1)
                W("scalar", s_te, te_q[h])
                if h >= 2:
                    W("scalar", s_ve, ve_q[h - 2])
                DO("scalar", lambda e, h=h, sl=sl: e.activation(
                    qf[sl][:, :], PS[:, sl * 1024:sl * 1024 + 1024],
                    AF.Identity, bias=bq_sb[:, h:h + 1]), inc=[(s_se, 1)])
                se_q[h] = C(s_se)
                W("sync", s_se, se_q[h])
                if h >= 2:
                    W("sync", s_ve, ve_q[h - 2])
                DO("sync", lambda e, sl=sl: e.dma_start(
                    qrot[sl][0:64, :], qf[sl][64:128, :]), inc=[(s_qr[sl], 16)])
                DO("sync", lambda e, sl=sl: e.dma_start(
                    qrot[sl][64:128, :], qf[sl][0:64, :]), inc=[(s_qr[sl], 16)])
                if ks < 8:
                    g, slc = ks // 4, ks % 4
                    cb = g * S + slc * 1024
                    if ks >= 2:
                        W("sync", s_ve, ve_krope[ks - 2])
                    DO("sync", lambda e, cb=cb, ks=ks: e.dma_start(
                        krot[ks % 2][0:64, :], ktb16[64:128, cb:cb + 1024]),
                       inc=[(s_kr, 16)])
                    DO("sync", lambda e, cb=cb, ks=ks: e.dma_start(
                        krot[ks % 2][64:128, :], ktb16[0:64, cb:cb + 1024]),
                       inc=[(s_kr, 16)])
                kr_cnt = C(s_kr)
                W("vector", s_qr[sl], C(s_qr[sl]))
                W("vector", s_se, se_q[h])
                if h == 0:
                    W("vector", s_const, C(s_const))
                DO("vector", lambda e, sl=sl: e.tensor_tensor(
                    t1_t[:, :], qf[sl][:, :], cosq_sb[:, :], ALU.mult))
                DO("vector", lambda e, sl=sl: e.tensor_tensor(
                    t2_t[:, :], qrot[sl][:, :], sinq_sb[:, :], ALU.mult))
                DR("vector")
                DO("vector", lambda e, h=h: e.tensor_tensor(
                    qt[:, h * TOK:(h + 1) * TOK], t1_t[:, :], t2_t[:, :],
                    ALU.add))
                DR("vector")
                if h >= 4:
                    W("vector", s_te, te_qn_map[h - 4])
                DO("vector", lambda e, h=h: e.tensor_tensor(
                    sq_bf[h % 4][:, :], qt[:, h * TOK:(h + 1) * TOK],
                    qt[:, h * TOK:(h + 1) * TOK], ALU.mult))
                ve_q[h] = vemark()
                if ks < 8:
                    g, slc = ks // 4, ks % 4
                    cb = g * S + slc * 1024
                    W("vector", s_kr, kr_cnt)
                    if h == 0:
                        W("vector", s_const, C(s_const))
                        W("vector", s_se, se_kv_last)
                    DO("vector", lambda e, cb=cb, slc=slc: e.tensor_tensor(
                        t1_t[:, :], ktb16[:, cb:cb + 1024],
                        cosk_sb[:, slc * 1024:slc * 1024 + 1024], ALU.mult))
                    DO("vector", lambda e, slc=slc, ks=ks: e.tensor_tensor(
                        t2_t[:, :], krot[ks % 2][:, :],
                        sink_sb[:, slc * 1024:slc * 1024 + 1024], ALU.mult))
                    DR("vector")
                    DO("vector", lambda e, cb=cb: e.tensor_tensor(
                        ktbf[:, cb:cb + 1024], t1_t[:, :], t2_t[:, :], ALU.add))
                    DR("vector")
                    if ks >= 2:
                        W("vector", s_te, te_kn_map[ks - 2])
                    DO("vector", lambda e, cb=cb, ks=ks: e.tensor_tensor(
                        sqk_bf[ks % 2][:, :], ktbf[:, cb:cb + 1024],
                        ktbf[:, cb:cb + 1024], ALU.mult))
                    ve_krope[ks] = vemark()
                if h >= 3:
                    emit_qn_dve(h - 3)
                if h >= 1 and (h - 1) < 8:
                    emit_kn_dve(h - 1)
            for hq in (NH - 3, NH - 2, NH - 1):
                emit_qn_pe(hq)
                emit_qn_dve(hq)

            DR("vector")
            for g in range(G):
                DO("vector", lambda e, g=g: e.tensor_scalar(
                    nct[0:1, g * 8:g * 8 + 8], qmax[0:1, g * 8:g * 8 + 8],
                    kmax[0:1, g:g + 1], None, ALU.mult))
            ve_prod = vemark()
            W("scalar", s_ve, ve_prod)
            DO("scalar", lambda e: e.activation(
                nct[0:1, 16:32], nct[0:1, 0:16], AF.Ln))
            DR("scalar")
            DO("scalar", lambda e: e.activation(
                nct[0:1, 32:48], nct[0:1, 16:32], AF.Exp, scale=0.5),
               inc=[(s_se, 1)])
            se_cst = C(s_se)
            W("vector", s_se, se_cst)
            DO("vector", lambda e: e.tensor_scalar(
                nct[0:1, 48:64], nct[0:1, 32:48], -(1.03 / 128.0), None,
                ALU.mult))
            ve_ncst = vemark()
            W("tensor", s_ve, ve_ncst)
            DO("tensor", lambda e: e.matmul(
                PS[:, 0:16], ones1f[:, :], nct[0:1, 48:64], start=True,
                stop=True))
            te_negc = temark()
            W("vector", s_te, te_negc)
            DO("vector", lambda e: e.tensor_copy(negc[:, :], PS[:, 0:16]))
            ve_negc = vemark()

            barrier()
            PL = 10

            def slot(kt):
                return kt if kt < 2 else 2 + (kt - 2) % 10

            ve_out = {}
            ve_invs_prev = 0
            for h in range(NH):
                g = h // 8
                te_qk, te_pv = {}, {}
                se0 = C(s_se)
                se_exp = {kt: se0 + kt + 1 for kt in range(NKT)}
                ve0 = C(s_ve)
                ve_rs = {kt: ve0 + kt // 4 + 1 for kt in range(NKT)}
                spair = (0, 1024, 3072)
                for kt in range(NKT + PL):
                    if kt < NKT:
                        if kt == 2 and h >= 1:
                            W("tensor", s_ve, ve_invs_prev)
                        if kt >= 3:
                            W("tensor", s_se, se_exp[kt - 3])
                        for half in range(2):
                            DO("tensor", lambda e, h=h, g=g, kt=kt, half=half: e.matmul(
                                PS[:, spair[kt % 3] + half * 512:spair[kt % 3] + half * 512 + 512],
                                ktbf[:, g * S + kt * 128:g * S + kt * 128 + 128],
                                qt[:, h * TOK + half * 512:h * TOK + half * 512 + 512],
                                start=True, stop=True))
                        te_qk[kt] = temark()
                    k2 = kt - PL
                    if k2 >= 0:
                        if kt >= NKT:
                            W("tensor", s_se, se_exp[k2])
                        if k2 == 0 and h >= 1:
                            W("tensor", s_ve, ve_out[h - 1])
                        for half in range(2):
                            DO("tensor", lambda e, g=g, k2=k2, half=half: e.matmul(
                                PS[:, 2048 + half * 512:2048 + half * 512 + 512],
                                vn[:, (g * NKT + k2) * 128:(g * NKT + k2) * 128 + 128],
                                et[slot(k2)][:, half * 512:half * 512 + 512],
                                start=(k2 == 0), stop=(k2 == NKT - 1)))
                        te_pv[k2] = temark()
                for kt in range(NKT):
                    W("scalar", s_te, te_pv[kt - PL] if kt >= PL else te_qk[kt])
                    if kt >= PL:
                        W("scalar", s_ve, ve_rs[kt - PL])
                    DO("scalar", lambda e, h=h, kt=kt: e.activation(
                        et[slot(kt)][:, :],
                        PS[:, spair[kt % 3]:spair[kt % 3] + 1024],
                        AF.Exp, bias=negc[:, h:h + 1], scale=1.0 / 128.0),
                       inc=[(s_se, 1)])
                    assert se_exp[kt] == C(s_se)
                for grp in range(8):
                    k0 = 4 * grp
                    W("vector", s_se, se_exp[k0 + 3])
                    DO("vector", lambda e, k0=k0: e.tensor_tensor(
                        tb1[:, :], et[slot(k0)][:, :], et[slot(k0 + 1)][:, :],
                        ALU.add))
                    DO("vector", lambda e, k0=k0: e.tensor_tensor(
                        tb2[:, :], et[slot(k0 + 2)][:, :],
                        et[slot(k0 + 3)][:, :], ALU.add))
                    DR("vector")
                    DO("vector", lambda e: e.tensor_tensor(
                        tb3[:, :], tb1[:, :], tb2[:, :], ALU.add))
                    DR("vector")
                    if grp == 0:
                        DO("vector", lambda e: e.tensor_copy(
                            rspart[:, :], tb3[:, :]))
                    else:
                        DO("vector", lambda e: e.tensor_tensor(
                            rspart[:, :], rspart[:, :], tb3[:, :], ALU.add))
                    DR("vector")
                    DO("vector", lambda e: e.engine_nop(), inc=[(s_ve, 1)])
                    assert ve_rs[k0 + 3] == C(s_ve)
                W("tensor", s_ve, ve_rs[NKT - 1])
                W("tensor", s_se, se_exp[NKT - 1])
                for half in range(2):
                    DO("tensor", lambda e, half=half: e.matmul(
                        PS[0:1, 3072 + half * 512:3072 + half * 512 + 512],
                        onesf[:, :], rspart[:, half * 512:half * 512 + 512],
                        start=True, stop=True))
                te_rs = temark()
                W("scalar", s_te, te_rs)
                DO("scalar", lambda e: e.activation(
                    rspart[0:1, 0:1024], PS[0:1, 3072:4096], AF.Ln))
                DR("scalar")
                DO("scalar", lambda e: e.activation(
                    invsb[0:1, 0:1024], rspart[0:1, 0:1024], AF.Exp,
                    scale=-1.0), inc=[(s_se, 1)])
                se_inv = C(s_se)
                W("tensor", s_se, se_inv)
                for half in range(2):
                    DO("tensor", lambda e, half=half: e.matmul(
                        PS[:, 3072 + half * 512:3072 + half * 512 + 512],
                        ones1f[:, :],
                        invsb[0:1, half * 512:half * 512 + 512],
                        start=True, stop=True))
                te_bc = temark()
                W("vector", s_te, te_bc)
                DO("vector", lambda e: e.tensor_copy(rspart[:, :], PS[:, 3072:4096]))
                ve_invs_prev = vemark()
                DR("vector")
                DO("vector", lambda e, h=h: e.tensor_tensor(
                    aoT[:, h * TOK:(h + 1) * TOK], PS[:, 2048:3072],
                    rspart[:, :], ALU.mult))
                ve_out[h] = vemark()

            barrier()
            te_wo, se_wo, ve_x1, te_st3 = {}, {}, {}, {}
            for dt in range(DCH):
                sl = dt % 2
                if dt >= 2:
                    W("gpsimd", s_te, te_wo[dt - 2])
                DO("gpsimd", lambda e, dt=dt, sl=sl: e.dma_start(
                    wo_bf[sl][:, :].rearrange("p (hc c) -> p hc c", hc=DCH),
                    wo_r[dt]), inc=[(s_wq[sl], 16)])
                if dt >= 2:
                    W("sync", s_ve, ve_x1[dt - 2])
                DO("sync", lambda e, dt=dt, sl=sl: e.dma_start(
                    xo_f[sl][:, :], x_own[dt * 128:(dt + 1) * 128, :]),
                   inc=[(s_xof[sl], 16)])
                W("tensor", s_wq[sl], C(s_wq[sl]))
                if dt >= 2:
                    W("tensor", s_se, se_wo[dt - 2])
                for hc in range(DCH):
                    for half in range(2):
                        DO("tensor", lambda e, hc=hc, half=half, sl=sl: e.matmul(
                            PS[:, sl * 1024 + half * 512:sl * 1024 + half * 512 + 512],
                            wo_bf[sl][:, hc * 128:(hc + 1) * 128],
                            aoT[:, hc * TOK + half * 512:hc * TOK + half * 512 + 512],
                            start=(hc == 0), stop=(hc == DCH - 1)))
                te_wo[dt] = temark()
                W("scalar", s_te, te_wo[dt])
                if dt >= 2:
                    W("scalar", s_ve, ve_x1[dt - 2])
                DO("scalar", lambda e, dt=dt, sl=sl: e.activation(
                    tmp_f[sl][:, :], PS[:, sl * 1024:sl * 1024 + 1024],
                    AF.Identity, bias=bo_sb[:, dt:dt + 1]), inc=[(s_se, 1)])
                se_wo[dt] = C(s_se)
                W("vector", s_se, se_wo[dt])
                W("vector", s_xof[sl], C(s_xof[sl]))
                DO("vector", lambda e, dt=dt, sl=sl: e.tensor_tensor(
                    x1[:, dt * TOK:(dt + 1) * TOK], tmp_f[sl][:, :],
                    xo_f[sl][:, :], ALU.add))
                DR("vector")
                if dt >= 2:
                    W("vector", s_te, te_st3[dt - 2])
                DO("vector", lambda e, dt=dt, sl=sl: e.tensor_tensor(
                    sq_t[sl][:, :], x1[:, dt * TOK:(dt + 1) * TOK],
                    x1[:, dt * TOK:(dt + 1) * TOK], ALU.mult))
                DR("vector")
                DO("vector", lambda e, dt=dt, sl=sl: e.tensor_copy(
                    xc_bf[sl][:, :], x1[:, dt * TOK:(dt + 1) * TOK]))
                ve_x1[dt] = vemark()

                def emit_st3(dq):
                    sq = dq % 2
                    W("tensor", s_ve, ve_x1[dq])
                    for half in range(2):
                        DO("tensor", lambda e, sq=sq, half=half, dq=dq: e.matmul(
                            PS[0:1, 2048 + half * 512:2048 + half * 512 + 512],
                            ones_bf[:, :],
                            xc_bf[sq][:, half * 512:half * 512 + 512],
                            start=(dq == 0), stop=(dq == DCH - 1)))
                    for half in range(2):
                        DO("tensor", lambda e, sq=sq, half=half, dq=dq: e.matmul(
                            PS[0:1, 3072 + half * 512:3072 + half * 512 + 512],
                            ones_bf[:, :], sq_t[sq][:, half * 512:half * 512 + 512],
                            start=(dq == 0), stop=(dq == DCH - 1)))
                    te_st3[dq] = temark()

                if dt >= 1:
                    emit_st3(dt - 1)
                if dt == DCH - 1:
                    emit_st3(dt)

            def ln_tail():
                W("vector", s_te, C(s_te))
                DO("vector", lambda e: e.tensor_scalar(
                    lt1[0:1, :], PS[0:1, 2048:3072], 1.0 / float(D),
                    None, ALU.mult))
                DO("vector", lambda e: e.tensor_scalar(
                    lt2[0:1, :], PS[0:1, 3072:4096], 1.0 / float(D),
                    None, ALU.mult))
                DR("vector")
                DO("vector", lambda e: e.tensor_tensor(
                    lt3[0:1, :], lt1[0:1, :], lt1[0:1, :],
                    ALU.mult))
                DR("vector")
                DO("vector", lambda e: e.tensor_tensor(
                    lt2[0:1, :], lt2[0:1, :], lt3[0:1, :],
                    ALU.subtract))
                vemark()
                W("scalar", s_ve, C(s_ve))
                DO("scalar", lambda e: e.activation(
                    lt3[0:1, :], lt2[0:1, :], AF.Ln,
                    bias=epsv[0:1, 0:1]))
                DR("scalar")
                DO("scalar", lambda e: e.activation(
                    lt2[0:1, :], lt3[0:1, :], AF.Exp, scale=-0.5),
                   inc=[(s_se, 1)])
                W("vector", s_se, C(s_se))
                DO("vector", lambda e: e.tensor_tensor(
                    lt1[0:1, :], lt1[0:1, :], lt2[0:1, :],
                    ALU.mult))
                DR("vector")
                DO("vector", lambda e: e.tensor_scalar(
                    lt1[0:1, :], lt1[0:1, :], -1.0, None, ALU.mult))
                vemark()
                W("tensor", s_ve, C(s_ve))
                for half in range(2):
                    DO("tensor", lambda e, half=half: e.matmul(
                        PS[:, half * 512:half * 512 + 512], ones1f[:, :],
                        lt2[0:1, half * 512:half * 512 + 512],
                        start=True, stop=True))
                for half in range(2):
                    DO("tensor", lambda e, half=half: e.matmul(
                        PS[:, 1024 + half * 512:1024 + half * 512 + 512],
                        ones1f[:, :], lt1[0:1, half * 512:half * 512 + 512],
                        start=True, stop=True))
                temark()
                W("vector", s_te, C(s_te))
                DO("vector", lambda e: e.tensor_copy(a_sb[:, :], PS[:, 0:1024]))
                DO("vector", lambda e: e.tensor_copy(b_sb[:, :], PS[:, 1024:2048]))
                vemark()

            ln_tail()
            ve_ln1 = {}
            for dt in range(DCH):
                sl = dt % 2
                DO("vector", lambda e, dt=dt, sl=sl: e.tensor_tensor(
                    tmp_f[sl][:, :], x1[:, dt * TOK:(dt + 1) * TOK], a_sb[:, :],
                    ALU.mult))
                DR("vector")
                DO("vector", lambda e, dt=dt, sl=sl: e.tensor_tensor(
                    x1[:, dt * TOK:(dt + 1) * TOK], tmp_f[sl][:, :], b_sb[:, :],
                    ALU.add))
                DR("vector")
                DO("vector", lambda e, dt=dt: e.tensor_copy(
                    x1b[:, dt * TOK:(dt + 1) * TOK],
                    x1[:, dt * TOK:(dt + 1) * TOK]))
                ve_ln1[dt] = vemark()
                W("tensor", s_ve, ve_ln1[dt])
                for half in range(2):
                    DO("tensor", lambda e, dt=dt, half=half: e.matmul(
                        PS[0:16, half * 512:half * 512 + 512],
                        gwt_sb[:, dt * 16:(dt + 1) * 16],
                        x1[:, dt * TOK + half * 512:dt * TOK + half * 512 + 512],
                        start=(dt == 0), stop=(dt == DCH - 1)))
            te_lg_pre = temark()

            if maybe_stop("ph3"):
                return
            barrier()
            state = {"sg_last": 0, "h_last": 0, "ce_last": 0, "y_last": {},
                     "slabfree": {}, "dslabfree": {}, "y_gate": 0,
                     "comb_last": 0}

            W("vector", s_te, te_lg_pre)
            DO("vector", lambda e: e.tensor_copy(lg_sb[:, :], PS[0:16, 0:1024]))
            ve_lg = vemark()
            ve_lgt = {}
            te_ftr = {}
            for tt in range(8):
                pb = 2 + tt % 2
                if tt == 0:
                    W("tensor", s_ve, ve_lg)
                if tt >= 2:
                    W("tensor", s_ve, ve_lgt[tt - 2])
                DO("tensor", lambda e, tt=tt, pb=pb: e.transpose(
                    PS[:, pb * 512:pb * 512 + 16],
                    lg_sb[0:16, tt * 128:(tt + 1) * 128], idf_sb[0:16, 0:16]))
                te_ftr[tt] = temark()
                W("vector", s_te, te_ftr[tt])
                DO("vector", lambda e, tt=tt, pb=pb: e.tensor_copy(
                    lgt8[:, tt * 16:(tt + 1) * 16], PS[:, pb * 512:pb * 512 + 16]))
                ve_lgt[tt] = vemark()
            state["h_last"] = ve_lgt[7]
            state["sg_last"] = C(s_se)

            ve_nm1 = {}
            ve_cmb = {}
            se_rexp = {}
            te_rtr = {}

            def hook_scalar(j):
                tt = j - 1
                if 0 <= tt < 8:
                    W("scalar", s_ve, ve_nm1[tt])
                    if tt >= 1:
                        W("scalar", s_ve, ve_cmb[tt - 1])
                    DO("scalar", lambda e, tt=tt: e.activation(
                        esc_t[:, :], lgt8[:, tt * 16:(tt + 1) * 16], AF.Exp,
                        bias=nm18[:, tt * 16:tt * 16 + 1]), inc=[(s_se, 1)])
                    se_rexp[tt] = C(s_se)

            def hook_vector(j):
                tt = j
                if tt < 8:
                    DO("vector", lambda e, tt=tt: e.reduce_max(
                        m1c[:, :], lgt8[:, tt * 16:(tt + 1) * 16], axis=AX.X))
                    DR("vector")
                    DO("vector", lambda e, tt=tt: e.tensor_scalar(
                        nm18[:, tt * 16:tt * 16 + 1], m1c[:, :], -1.0, None,
                        ALU.mult))
                    ve_nm1[tt] = vemark()
                tt = j - 1
                if 0 <= tt < 8:
                    W("vector", s_se, se_rexp[tt])
                    DO("vector", lambda e: e.reduce_sum(rsum_t[:, :], esc_t[:, :],
                                                        axis=AX.X))
                    DR("vector")
                    DO("vector", lambda e: e.reciprocal(rinv_t[:, :], rsum_t[:, :]))
                    DR("vector")
                    DO("vector", lambda e: e.tensor_scalar(
                        sc_t[:, :], esc_t[:, :], rinv_t[:, :], None, ALU.mult))
                    DR("vector")
                    DO("vector", lambda e: e.reduce_max(m1c[:, :], sc_t[:, :],
                                                        axis=AX.X))
                    DR("vector")
                    DO("vector", lambda e: e.tensor_scalar(
                        eqm_t[:, :], sc_t[:, :], m1c[:, :], None, ALU.is_equal))
                    DR("vector")
                    DO("vector", lambda e: e.tensor_scalar(
                        big_t[:, :], eqm_t[:, :], 1e30, None, ALU.mult))
                    DR("vector")
                    DO("vector", lambda e: e.tensor_tensor(
                        msk_t[:, :], sc_t[:, :], big_t[:, :], ALU.subtract))
                    DR("vector")
                    DO("vector", lambda e: e.reduce_max(m2c[:, :], msk_t[:, :],
                                                        axis=AX.X))
                    DR("vector")
                    DO("vector", lambda e: e.tensor_scalar(
                        keep_t[:, :], sc_t[:, :], m2c[:, :], None, ALU.is_ge))
                    DR("vector")
                    DO("vector", lambda e, tt=tt: e.tensor_tensor(
                        cmb8[:, tt * 16:(tt + 1) * 16], sc_t[:, :], keep_t[:, :],
                        ALU.mult))
                    ve_cmb[tt] = vemark()

            def hook_tensor_post(j):
                tt = j - 2
                if 0 <= tt < 8:
                    W("tensor", s_ve, ve_cmb[tt])
                    DO("tensor", lambda e, tt=tt: e.transpose(
                        PS[0:16, 2048:2176], cmb8[:, tt * 16:(tt + 1) * 16],
                        idf_sb[:, :]))
                    te_rtr[tt] = temark()

            def hook_vector_post(j):
                tt = j - 2
                if 0 <= tt < 8:
                    W("vector", s_te, te_rtr[tt])
                    DO("vector", lambda e, tt=tt: e.tensor_copy(
                        comb_sb[0:16, tt * 128:(tt + 1) * 128],
                        PS[0:16, 2048:2176]))
                    state["comb_last"] = vemark()
                    state["y_gate"] = state["comb_last"]

            def gu_block(tag, slab_src, use_ce, hooks=False):
                ve_h_tiles = {}
                for j in range(NJ):
                    sl = j % 2
                    key = (tag, j - 2)
                    if key in state["slabfree"]:
                        W("gpsimd", s_te, state["slabfree"][key])
                    DO("gpsimd", lambda e, j=j, sl=sl: e.dma_start(
                        wgu_bf[sl][:, :].rearrange("p (u f) -> p u f", u=2),
                        slab_src(j)), inc=[(s_wg[sl], 16)])
                    W("tensor", s_wg[sl], C(s_wg[sl]))
                    W("tensor", s_se, state["sg_last"])
                    for hc in range(DCH):
                        for half in range(2):
                            DO("tensor", lambda e, hc=hc, half=half, sl=sl: e.matmul(
                                PS[:, half * 512:half * 512 + 512],
                                wgu_bf[sl][:, hc * 128:(hc + 1) * 128],
                                x1b[:, hc * TOK + half * 512:hc * TOK + half * 512 + 512],
                                start=(hc == 0), stop=(hc == DCH - 1)))
                    te_g = temark()
                    W("tensor", s_ve, state["h_last"])
                    if use_ce:
                        W("tensor", s_ve, state["ce_last"])
                    for hc in range(DCH):
                        for half in range(2):
                            DO("tensor", lambda e, hc=hc, half=half, sl=sl: e.matmul(
                                PS[:, 1024 + half * 512:1024 + half * 512 + 512],
                                wgu_bf[sl][:, D + hc * 128:D + hc * 128 + 128],
                                x1b[:, hc * TOK + half * 512:hc * TOK + half * 512 + 512],
                                start=(hc == 0), stop=(hc == DCH - 1)))
                    te_u = temark()
                    state["slabfree"][(tag, j)] = te_u
                    if hooks:
                        hook_scalar(j)
                    W("scalar", s_te, te_g)
                    if j >= 2:
                        W("scalar", s_ve, ve_h_tiles[j - 2])
                    DO("scalar", lambda e, sl=sl: e.activation(
                        sg_bf[sl][:, :], PS[:, 0:1024], AF.Silu),
                       inc=[(s_se, 1)])
                    state["sg_last"] = C(s_se)
                    W("vector", s_te, te_u)
                    if use_ce:
                        DO("vector", lambda e, sl=sl: e.tensor_tensor(
                            uf_sb[sl][:, :], PS[:, 1024:2048], ce_sb[:, :],
                            ALU.mult))
                    else:
                        DO("vector", lambda e, sl=sl: e.tensor_copy(
                            uf_sb[sl][:, :], PS[:, 1024:2048]))
                    DR("vector")
                    W("vector", s_se, state["sg_last"])
                    DO("vector", lambda e, j=j, sl=sl: e.tensor_tensor(
                        hpr[:, j * TOK:(j + 1) * TOK], sg_bf[sl][:, :],
                        uf_sb[sl][:, :], ALU.mult))
                    ve_h_tiles[j] = vemark()
                    state["h_last"] = ve_h_tiles[j]
                    if hooks:
                        hook_vector(j)
                        hook_tensor_post(j)
                        hook_vector_post(j)

            def down_block(tag, slab_src):
                h_ready = state["h_last"]
                for hp in range(8):
                    sl = hp % 2
                    key = (tag, hp - 2)
                    if key in state["dslabfree"]:
                        W("gpsimd", s_te, state["dslabfree"][key])
                    DO("gpsimd", lambda e, hp=hp, sl=sl: e.dma_start(
                        wd_bf[sl][:, :].rearrange("p (u f) -> p u f", u=2),
                        slab_src(hp)), inc=[(s_wd[sl], 16)])
                    W("tensor", s_wd[sl], C(s_wd[sl]))
                    if hp == 0:
                        W("tensor", s_ve, h_ready)
                        W("tensor", s_ve, state["y_gate"])
                    for tp in range(2):
                        ht = hp * 2 + tp
                        yb = 4 + (ht % 2) * 2
                        if ht >= 2:
                            W("tensor", s_ve, state["y_last"][(tag, ht - 2)])
                        for ic in range(NJ):
                            for half in range(2):
                                DO("tensor", lambda e, tp=tp, ic=ic, half=half, sl=sl, yb=yb: e.matmul(
                                    PS[:, yb * 512 + half * 512:yb * 512 + half * 512 + 512],
                                    wd_bf[sl][:, tp * NJ * 128 + ic * 128:tp * NJ * 128 + ic * 128 + 128],
                                    hpr[:, ic * TOK + half * 512:ic * TOK + half * 512 + 512],
                                    start=(ic == 0), stop=(ic == NJ - 1)))
                        te_y = temark()
                        W("vector", s_te, te_y)
                        DO("vector", lambda e, ht=ht, yb=yb: e.tensor_tensor(
                            x1[:, ht * TOK:(ht + 1) * TOK],
                            PS[:, yb * 512:yb * 512 + 1024],
                            x1[:, ht * TOK:(ht + 1) * TOK], ALU.add))
                        state["y_last"][(tag, ht)] = vemark()
                    state["dslabfree"][(tag, hp)] = C(s_te)

            for r in range(2):
                tag = "sh%d" % r
                gu_block(tag,
                         lambda j, r=r: sgu_r[r * NJ + j].rearrange(
                             "u p h c -> p u (h c)"),
                         use_ce=False, hooks=(r == 0))
                down_block(tag,
                           lambda hp, r=r: sd_r[hp][:, :, r * NJ:(r + 1) * NJ, :]
                           .rearrange("u p i c -> p u (i c)"))

            for ex in range(NE):
                tag = "e%d" % ex
                W("tensor", s_ve, state["h_last"])
                W("tensor", s_ve, state["comb_last"])
                for half in range(2):
                    DO("tensor", lambda e, ex=ex, half=half: e.matmul(
                        PS[:, 1024 + half * 512:1024 + half * 512 + 512],
                        sel_sb[:, ex * 128:(ex + 1) * 128],
                        comb_sb[0:16, half * 512:half * 512 + 512],
                        start=True, stop=True))
                te_ce = temark()
                W("vector", s_te, te_ce)
                DO("vector", lambda e: e.tensor_copy(ce_sb[:, :], PS[:, 1024:2048]))
                state["ce_last"] = vemark()
                gu_block(tag,
                         lambda j, ex=ex: wgu_r[ex, j].rearrange(
                             "u p h c -> p u (h c)"),
                         use_ce=True)
                down_block(tag,
                           lambda hp, ex=ex: wd_r[ex, hp].rearrange(
                               "u p i c -> p u (i c)"))

            if maybe_stop("ph6"):
                return
            barrier()
            te_st7 = {}
            for dt in range(DCH):
                sl = dt % 2
                if dt >= 2:
                    W("vector", s_te, te_st7[dt - 2])
                DO("vector", lambda e, dt=dt, sl=sl: e.tensor_tensor(
                    tmp7[sl][:, :], x1[:, dt * TOK:(dt + 1) * TOK],
                    x1[:, dt * TOK:(dt + 1) * TOK], ALU.mult))
                ve_sq7 = vemark()
                W("tensor", s_ve, ve_sq7)
                for half in range(2):
                    DO("tensor", lambda e, dt=dt, half=half: e.matmul(
                        PS[0:1, 2048 + half * 512:2048 + half * 512 + 512],
                        onesf[:, :],
                        x1[:, dt * TOK + half * 512:dt * TOK + half * 512 + 512],
                        start=(dt == 0), stop=(dt == DCH - 1)))
                for half in range(2):
                    DO("tensor", lambda e, sl=sl, half=half, dt=dt: e.matmul(
                        PS[0:1, 3072 + half * 512:3072 + half * 512 + 512],
                        onesf[:, :], tmp7[sl][:, half * 512:half * 512 + 512],
                        start=(dt == 0), stop=(dt == DCH - 1)))
                te_st7[dt] = temark()
            ln_tail()
            for dt in range(DCH):
                sl = dt % 2
                DO("vector", lambda e, dt=dt, sl=sl: e.tensor_tensor(
                    tmp7[sl][:, :], x1[:, dt * TOK:(dt + 1) * TOK], a_sb[:, :],
                    ALU.mult))
                DR("vector")
                DO("vector", lambda e, dt=dt, sl=sl: e.tensor_tensor(
                    x1[:, dt * TOK:(dt + 1) * TOK], tmp7[sl][:, :], b_sb[:, :],
                    ALU.add))
                vemark()
            W("sync", s_ve, C(s_ve))
            DO("sync", lambda e: e.dma_start(
                out.rearrange("(dc p) t -> p dc t", p=128),
                x1[:, :].rearrange("p (dc t) -> p dc t", dc=DCH)),
               inc=[(s_st, 16)])
            W("sync", s_st, C(s_st))

        for name in ENGINES:
            def body(eng, name=name):
                pp.reset(name, eng)
                emit(pp)
            getattr(blk, name)(body)

    return nc


def _prep(inputs):
    f32 = np.float32
    c = np.ascontiguousarray
    x = np.asarray(inputs["L0_attn_x"], f32)
    wq = np.asarray(inputs["L0_attn_wq"], f32)
    bq = np.asarray(inputs["L0_attn_bq"], f32)
    wk = np.asarray(inputs["L0_attn_wk"], f32)
    bk = np.asarray(inputs["L0_attn_bk"], f32)
    wv = np.asarray(inputs["L0_attn_wv"], f32)
    bv = np.asarray(inputs["L0_attn_bv"], f32)
    wo = np.asarray(inputs["L0_attn_wo"], f32)
    bo = np.asarray(inputs["L0_attn_bo"], f32)
    gate_w = np.asarray(inputs["L0_ffn_gate_w"], f32)
    wgu = np.asarray(inputs["L0_ffn_w_gate_up"], f32)
    wd = np.asarray(inputs["L0_ffn_w_down"], f32)
    sgu = np.asarray(inputs["L0_ffn_shared_gate_up"], f32)
    sd = np.asarray(inputs["L0_ffn_shared_down"], f32)

    xT = c(x.transpose(0, 2, 1))
    wq_r = c(wq.reshape(DCH, 128, NH, 128).transpose(2, 1, 0, 3))
    wk_r = c(wk.reshape(DCH, 128, G, 128).transpose(2, 1, 0, 3))
    wv_r = c(wv.reshape(DCH, 128, G, 128).transpose(2, 1, 0, 3))
    wo_r = c(wo.reshape(DCH, 128, DCH, 128).transpose(2, 1, 0, 3))
    bq_r = c(bq.reshape(NH, 128).T)
    bk_r = c(bk.reshape(G, 128).T)
    bv_row = c(bv.reshape(1, G * 128))
    bo_r = c(bo.reshape(DCH, 128).T)
    gwt = c(gate_w.T)
    wg5 = wgu.reshape(NE, DCH, 128, 2 * NJ, 128).transpose(0, 3, 2, 1, 4)
    wgu_r = c(np.stack([wg5[:, :NJ], wg5[:, NJ:]], axis=2))
    wd5 = wd.reshape(NE, NJ, 128, DCH, 128).transpose(0, 3, 2, 1, 4)
    wd_r = c(wd5.reshape(NE, 8, 2, 128, NJ, 128))
    sg4 = sgu.reshape(DCH, 128, 2 * SJ, 128).transpose(2, 1, 0, 3)
    sgu_r = c(np.stack([sg4[:SJ], sg4[SJ:]], axis=1))
    sd4 = sd.reshape(SJ, 128, DCH, 128).transpose(2, 1, 0, 3)
    sd_r = c(sd4.reshape(8, 2, 128, SJ, 128))

    pos = np.arange(S, dtype=np.float64)
    invf = 1.0 / (10000.0 ** (np.arange(0, HD, 2, dtype=np.float64) / HD))
    fr = np.outer(pos, invf)
    cos_t = np.cos(fr).T
    sin_t = np.sin(fr).T
    bf16 = ml_dtypes.bfloat16
    cosk = np.concatenate([cos_t, cos_t], axis=0).astype(bf16)
    sink = np.concatenate([-sin_t, sin_t], axis=0).astype(bf16)
    idf = np.eye(128, dtype=f32)
    selm = np.zeros((NE, NE * 128), f32)
    for e in range(NE):
        selm[e, e * 128:(e + 1) * 128] = 1.0

    shared = dict(wq_r=wq_r, wk_r=wk_r, wv_r=wv_r, wo_r=wo_r, bq_r=bq_r,
                  bk_r=bk_r, bv_row=bv_row, bo_r=bo_r, gwt=gwt, wgu_r=wgu_r,
                  wd_r=wd_r, sgu_r=sgu_r, sd_r=sd_r, cosk=cosk, sink=sink,
                  idf=idf, sel=selm)
    in_maps = []
    for core in range(8):
        b, qc = core // 4, core % 4
        m = dict(shared)
        m["x_batch"] = xT[b]
        m["x_own"] = c(xT[b][:, qc * TOK:(qc + 1) * TOK])
        m["cosq"] = c(cosk[:, qc * TOK:(qc + 1) * TOK])
        m["sinq"] = c(sink[:, qc * TOK:(qc + 1) * TOK])
        in_maps.append(m)
    return in_maps


def kernel(**inputs):
    in_maps = _prep(inputs)
    nc = build_program()
    res = run_bass_kernel_spmd(nc, in_maps, core_ids=list(range(8)))
    out = np.empty((2, S, D), np.float32)
    for core in range(8):
        b, qc = core // 4, core % 4
        out[b, qc * TOK:(qc + 1) * TOK, :] = res.results[core]["out"].T
    return out


# version 13 (final)
# speedup vs baseline: 1.0589x; 1.0124x over previous
import numpy as np
import ml_dtypes
from contextlib import ExitStack

import concourse.bass as bass
import concourse.mybir as mybir
from concourse.bass_utils import run_bass_kernel_spmd

F32 = mybir.dt.float32
BF16 = mybir.dt.bfloat16
AF = mybir.ActivationFunctionType
ALU = mybir.AluOpType
AX = mybir.AxisListType

D = 2048
S = 4096
TOK = 1024
NH = 16
G = 2
HD = 128
DCH = 16
NKT = 32
NE = 16
NJ = 11
SJ = 22
ENGINES = ("sync", "gpsimd", "tensor", "vector", "scalar")
KB = 1024


class P:

    def __init__(self):
        self.counts = {}
        self.active = None
        self.eng = None
        self.sems = []

    def sem(self, handle):
        self.counts[id(handle)] = 0
        self.sems.append(handle)
        return handle

    def reset(self, name, eng):
        self.active = name
        self.eng = eng
        for k in self.counts:
            self.counts[k] = 0

    def do(self, engine, thunk, inc=None):
        if self.active == engine:
            inst = thunk(self.eng)
            self.last_inst = inst
            if inc is not None:
                for s, n in inc:
                    inst.then_inc(s, n)
        if inc is not None:
            for s, n in inc:
                self.counts[id(s)] += n

    def mark(self, engine, sem, n=1):
        if self.active == engine:
            self.last_inst.then_inc(sem, n)
        self.counts[id(sem)] += n

    def wait(self, engine, sem, val=None):
        tgt = self.counts[id(sem)] if val is None else val
        if tgt <= 0:
            return
        if self.active == engine:
            self.eng.wait_ge(sem, tgt)

    def count(self, sem):
        return self.counts[id(sem)]

    def drain(self, engine):
        if self.active == engine:
            self.eng.drain()


def build_program(stop_after=None):
    nc = bass.Bass("TRN2", target_bir_lowering=False, detect_race_conditions=False)

    def din(name, shape, dt=F32):
        return nc.dram_tensor(name, list(shape), dt, kind="ExternalInput").ap()

    x_batch = din("x_batch", [D, S])
    x_own = din("x_own", [D, TOK])
    wq_r = din("wq_r", [NH, 128, DCH, 128])
    wk_r = din("wk_r", [G, 128, DCH, 128])
    wv_r = din("wv_r", [G, 128, DCH, 128])
    wo_r = din("wo_r", [DCH, 128, DCH, 128])
    bq_r = din("bq_r", [128, NH])
    bk_r = din("bk_r", [128, G])
    bv_row = din("bv_row", [1, G * 128])
    bo_r = din("bo_r", [128, DCH])
    gwt = din("gwt", [D, NE])
    wgu_r = din("wgu_r", [NE, NJ, 2, 128, DCH, 128])
    wd_r = din("wd_r", [NE, 8, 2, 128, NJ, 128])
    sgu_r = din("sgu_r", [SJ, 2, 128, DCH, 128])
    sd_r = din("sd_r", [8, 2, 128, SJ, 128])
    cosq = din("cosq", [128, TOK], BF16)
    sinq = din("sinq", [128, TOK], BF16)
    cosk = din("cosk", [128, S], BF16)
    sink = din("sink", [128, S], BF16)
    idf = din("idf", [128, 128])
    sel = din("sel", [NE, NE * 128])

    out = nc.dram_tensor("out", [D, TOK], F32, kind="ExternalOutput").ap()

    ARENA = 202 * KB
    arena = nc.alloc_sbuf_tensor("arena", [128, ARENA], mybir.dt.uint8)
    base = nc.lookup_mloc(arena).addr

    def at(name, shape, dt, off):
        assert off % 32 == 0, name
        need = int(np.prod(shape[1:])) * mybir.dt.size(dt)
        assert off + need <= ARENA, f"{name}: {off}+{need} > {ARENA}"
        return nc.alloc_sbuf_tensor_at(name, list(shape), dt, offset=base + off).ap()

    ones_bf = at("ones_bf", [128, 1], BF16, 0)
    ones1f = at("ones1f", [1, 128], F32, 32)
    ones1b = at("ones1b", [1, 128], BF16, 576)
    onesf = at("onesf", [128, 1], F32, 864)
    epsv = at("epsv", [1, 1], F32, 896)
    bq_sb = at("bq_sb", [128, NH], F32, 928)
    bk_sb = at("bk_sb", [128, G], F32, 992)
    bv_bf = at("bv_bf", [1, G * 128], BF16, 1024)
    bo_sb = at("bo_sb", [128, DCH], F32, 1568)
    qmax = at("qmax", [1, NH], F32, 1632)
    kmax = at("kmax", [1, G], F32, 1696)
    negc = at("negc", [128, NH], F32, 1760)
    gwt_sb = at("gwt_sb", [128, DCH * NE], F32, 1824)
    idf_sb = at("idf_sb", [128, 128], F32, 2848)
    comb_sb = at("comb_sb", [16, 8 * 128], F32, 3392)
    sel_sb = at("sel_sb", [NE, NE * 128], F32, 7488)
    sc1 = at("sc1", [1, 1], F32, 15680)
    qt = at("qt", [128, NH * TOK], BF16, 16 * KB)
    x1b = at("x1b", [128, DCH * TOK], BF16, 16 * KB)
    xb_bf = [at("xb_bf0", [128, DCH * 512], BF16, 48 * KB),
             at("xb_bf1", [128, DCH * 512], BF16, 64 * KB)]
    ktb16 = at("ktb16", [128, G * S], BF16, 144 * KB)
    cosk_sb = at("cosk_sb", [128, S], BF16, 168 * KB)
    sink_sb = at("sink_sb", [128, S], BF16, 176 * KB)
    x_own_bf = at("x_own_bf", [128, DCH * TOK], BF16, 48 * KB)
    wq_bf = [at("wq_bf0", [128, D], BF16, 80 * KB),
             at("wq_bf1", [128, D], BF16, 84 * KB)]
    qf = [at("qf0", [128, TOK], BF16, 88 * KB),
          at("qf1", [128, TOK], BF16, 90 * KB)]
    qrot = [at("qrot0", [128, TOK], BF16, 92 * KB),
            at("qrot1", [128, TOK], BF16, 94 * KB)]
    krot = [at("krot0", [128, TOK], BF16, 160 * KB),
            at("krot1", [128, TOK], BF16, 162 * KB)]
    t1_t = at("t1_t", [128, TOK], BF16, 104 * KB)
    t2_t = at("t2_t", [128, TOK], BF16, 106 * KB)
    nct = at("nct", [1, 64], F32, 108 * KB)
    et = [at("et%d" % i, [128, TOK], BF16, (88 + 2 * i) * KB)
          for i in range(12)]
    invsb = at("invsb", [128, TOK], F32, 152 * KB)
    tb1 = at("tb1", [128, TOK], BF16, 156 * KB)
    tb2 = at("tb2", [128, TOK], BF16, 158 * KB)
    tb3 = at("tb3", [128, TOK], BF16, 200 * KB)
    rspart = at("rspart", [128, TOK], F32, 196 * KB)
    x1 = at("x1", [128, DCH * TOK], F32, 48 * KB)
    vn = at("vn", [128, G * S], BF16, 112 * KB)
    ktbf = at("ktbf", [128, G * S], BF16, 128 * KB)
    wo_bf = [at("wo_bf0", [128, D], BF16, 112 * KB),
             at("wo_bf1", [128, D], BF16, 116 * KB)]
    tmp_f = [at("tmp_f0", [128, TOK], F32, 120 * KB),
             at("tmp_f1", [128, TOK], F32, 124 * KB)]
    xo_f = [at("xo_f0", [128, TOK], F32, 128 * KB),
            at("xo_f1", [128, TOK], F32, 132 * KB)]
    sq_t = [at("sq_t0", [128, TOK], BF16, 136 * KB),
            at("sq_t1", [128, TOK], BF16, 138 * KB)]
    xc_bf = [at("xc_bf0", [128, TOK], BF16, 140 * KB),
             at("xc_bf1", [128, TOK], BF16, 142 * KB)]
    lg_sb = at("lg_sb", [16, TOK], F32, 144 * KB)
    lgt_t = at("lgt_t", [128, 16], F32, 148 * KB)
    lgt8 = at("lgt8", [128, 8 * 16], F32, 148 * KB + 512)
    nm18 = at("nm18", [128, 8 * 16], F32, 149 * KB)
    cmb8 = at("cmb8", [128, 8 * 16], F32, 149 * KB + 512)
    m1c = at("m1c", [128, 1], F32, 150 * KB)
    nm1 = at("nm1", [128, 1], F32, 150 * KB + 64)
    rsum_t = at("rsum_t", [128, 1], F32, 150 * KB + 128)
    rinv_t = at("rinv_t", [128, 1], F32, 150 * KB + 192)
    m2c = at("m2c", [128, 1], F32, 150 * KB + 256)
    esc_t = at("esc_t", [128, 16], F32, 150 * KB + 320)
    sc_t = at("sc_t", [128, 16], F32, 150 * KB + 384)
    eqm_t = at("eqm_t", [128, 16], F32, 150 * KB + 448)
    big_t = at("big_t", [128, 16], F32, 150 * KB + 512)
    msk_t = at("msk_t", [128, 16], F32, 150 * KB + 576)
    keep_t = at("keep_t", [128, 16], F32, 150 * KB + 640)
    cmb_t = at("cmb_t", [128, 16], F32, 150 * KB + 704)
    wgu_bf = [at("wgu_bf0", [128, 2 * D], BF16, 112 * KB),
              at("wgu_bf1", [128, 2 * D], BF16, 120 * KB)]
    wd_bf = [at("wd_bf0", [128, 2 * NJ * 128], BF16, 128 * KB),
             at("wd_bf1", [128, 2 * NJ * 128], BF16, 134 * KB)]
    tmp7 = [at("tmp7_0", [128, TOK], F32, 112 * KB),
            at("tmp7_1", [128, TOK], F32, 116 * KB)]
    sq7 = [at("sq7_0", [128, TOK], BF16, 112 * KB),
           at("sq7_1", [128, TOK], BF16, 114 * KB)]
    wk_bf = at("wk_bf", [128, G * D], BF16, 184 * KB)
    wv_bf = at("wv_bf", [128, G * D], BF16, 192 * KB)
    sqk_bf = [at("sqk_bf0", [128, TOK], BF16, 164 * KB),
              at("sqk_bf1", [128, TOK], BF16, 166 * KB)]
    sq_bf = [at("sq_bf0", [128, TOK], BF16, 184 * KB),
             at("sq_bf1", [128, TOK], BF16, 186 * KB),
             at("sq_bf2", [128, TOK], BF16, 188 * KB),
             at("sq_bf3", [128, TOK], BF16, 194 * KB)]
    cosq_sb = at("cosq_sb", [128, TOK], BF16, 190 * KB)
    sinq_sb = at("sinq_sb", [128, TOK], BF16, 192 * KB)
    aoT = at("aoT", [128, NH * TOK], BF16, 160 * KB)
    a_sb = at("a_sb", [128, TOK], F32, 144 * KB)
    b_sb = at("b_sb", [128, TOK], F32, 148 * KB)
    lt1 = at("lt1b", [1, TOK], F32, 152 * KB)
    lt2 = at("lt2b", [1, TOK], F32, 156 * KB)
    lt3 = at("lt3b", [1, TOK], F32, 192 * KB)
    hpr = at("hpr", [128, NJ * TOK], BF16, 160 * KB)
    sg_bf = [at("sg_bf0", [128, TOK], BF16, 182 * KB),
             at("sg_bf1", [128, TOK], BF16, 184 * KB)]
    uf_sb = [at("uf_sb0", [128, TOK], F32, 186 * KB),
             at("uf_sb1", [128, TOK], F32, 190 * KB)]
    ce_sb = at("ce_sb", [128, TOK], F32, 194 * KB)

    es = ExitStack()
    with es:
        PS = es.enter_context(nc.psum_tensor("PS", [128, 4096], F32))

        sems = {}

        def mksem(name):
            sems[name] = es.enter_context(nc.semaphore(name))
            return sems[name]

        s_te = mksem("s_te")
        s_ve = mksem("s_ve")
        s_se = mksem("s_se")
        s_st = mksem("s_st")
        s_const = mksem("s_const")
        s_kv = mksem("s_kv")
        s_bvv = mksem("s_bvv")
        s_xb = [mksem("s_xb0"), mksem("s_xb1")]
        s_qr = [mksem("s_qr0"), mksem("s_qr1")]
        s_kr = mksem("s_kr")
        s_xobf = mksem("s_xobf")
        s_wq = [mksem("s_wq0"), mksem("s_wq1")]
        s_xof = [mksem("s_xof0"), mksem("s_xof1")]
        s_wg = [mksem("s_wg0"), mksem("s_wg1")]
        s_wd = [mksem("s_wd0"), mksem("s_wd1")]

        blk = es.enter_context(nc.Block())
        pp = P()
        for s in sems.values():
            pp.sem(s)

        def emit(p):
            W, C, DO, DR = p.wait, p.count, p.do, p.drain

            def bank(b, w=512):
                return PS[:, b * 512:b * 512 + w]

            def temark(eng="tensor"):
                p.mark(eng, s_te)
                return C(s_te)

            def vemark():
                DR("vector")
                DO("vector", lambda e: e.engine_nop(), inc=[(s_ve, 1)])
                return C(s_ve)

            def barrier():
                for ename in ENGINES:
                    for s in pp.sems:
                        if s is not s_st:
                            W(ename, s)

            def maybe_stop(name):
                if stop_after != name:
                    return False
                barrier()
                W("sync", s_ve, C(s_ve))
                DO("sync", lambda e: e.dma_start(
                    out.rearrange("(dc p) t -> p dc t", p=128),
                    x1[:, :].rearrange("p (dc t) -> p dc t", dc=DCH)),
                   inc=[(s_st, 16)])
                W("sync", s_st, C(s_st))
                return True

            for dst, src in ((bq_sb, bq_r), (bk_sb, bk_r), (bo_sb, bo_r),
                             (idf_sb, idf), (sel_sb, sel),
                             (cosk_sb, cosk), (sink_sb, sink)):
                DO("sync", lambda e, d=dst, s=src: e.dma_start(d[:, :], s),
                   inc=[(s_const, 16)])
            DO("sync", lambda e: e.dma_start(
                gwt_sb[:, :].rearrange("p (h e2) -> p h e2", h=DCH),
                gwt.rearrange("(h p) e2 -> p h e2", p=128)), inc=[(s_const, 16)])
            const_tot = C(s_const)
            for buf, val in ((ones_bf, 1.0), (ones1f, 1.0), (ones1b, 1.0),
                             (onesf, 1.0), (epsv, 1e-5), (qmax, 0.0),
                             (kmax, 0.0)):
                DO("vector", lambda e, b=buf, v=val: e.memset(b[:, :], v))
            ve_init = vemark()

            if maybe_stop("ph0"):
                return
            te_k, te_v, te_xbfree, se_kvev = {}, {}, {}, {}
            for ch in range(8):
                sl = ch % 2
                if ch >= 2:
                    W("gpsimd", s_te, te_xbfree[ch - 2])
                DO("gpsimd", lambda e, ch=ch, sl=sl: e.dma_start(
                    xb_bf[sl][:, :].rearrange("p (h t) -> p h t", h=DCH),
                    x_batch[:, ch * 512:(ch + 1) * 512]
                    .rearrange("(h p) t -> p h t", p=128)), inc=[(s_xb[sl], 16)])
                if ch == 0:
                    DO("gpsimd", lambda e: e.dma_start(
                        wk_bf[:, :].rearrange("p (g f) -> p g f", g=G),
                        wk_r.rearrange("g p h c -> p g (h c)")),
                       inc=[(s_kv, 16)])
                    DO("gpsimd", lambda e: e.dma_start(bv_bf[:, :], bv_row),
                       inc=[(s_bvv, 16)])
                    DO("gpsimd", lambda e: e.dma_start(
                        wv_bf[:, :].rearrange("p (g f) -> p g f", g=G),
                        wv_r.rearrange("g p h c -> p g (h c)")),
                       inc=[(s_bvv, 16)])
                W("tensor", s_xb[sl], C(s_xb[sl]))
                if ch == 0:
                    W("tensor", s_kv, 16)
                    W("tensor", s_ve, ve_init)
                for g in range(G):
                    kb, vb = g * 2 + sl, 4 + g * 2 + sl
                    if ch >= 2:
                        W("tensor", s_se, se_kvev[(ch - 2, g, 0)])
                    for hc in range(DCH):
                        DO("tensor", lambda e, g=g, hc=hc, sl=sl, kb=kb: e.matmul(
                            bank(kb),
                            wk_bf[:, g * D + hc * 128:g * D + hc * 128 + 128],
                            xb_bf[sl][:, hc * 512:(hc + 1) * 512],
                            start=(hc == 0), stop=(hc == DCH - 1)))
                    te_k[(ch, g)] = temark()
                    if ch == 0 and g == 0:
                        W("tensor", s_bvv, 32)
                    if ch >= 2:
                        W("tensor", s_se, se_kvev[(ch - 2, g, 1)])
                    for tt in range(4):
                        for hc in range(DCH):
                            DO("tensor", lambda e, g=g, hc=hc, tt=tt, sl=sl, vb=vb: e.matmul(
                                PS[:, vb * 512 + tt * 128:vb * 512 + tt * 128 + 128],
                                xb_bf[sl][:, hc * 512 + tt * 128:hc * 512 + tt * 128 + 128],
                                wv_bf[:, g * D + hc * 128:g * D + hc * 128 + 128],
                                start=(hc == 0), stop=False))
                        DO("tensor", lambda e, g=g, tt=tt, vb=vb: e.matmul(
                            PS[:, vb * 512 + tt * 128:vb * 512 + tt * 128 + 128],
                            ones1b[:, :], bv_bf[0:1, g * 128:(g + 1) * 128],
                            start=False, stop=True))
                    te_v[(ch, g)] = temark()
                te_xbfree[ch] = C(s_te)
                for g in range(G):
                    kb, vb = g * 2 + sl, 4 + g * 2 + sl
                    W("scalar", s_te, te_k[(ch, g)])
                    DO("scalar", lambda e, g=g, ch=ch, kb=kb: e.activation(
                        ktb16[:, g * S + ch * 512:g * S + (ch + 1) * 512],
                        bank(kb), AF.Identity, bias=bk_sb[:, g:g + 1]),
                       inc=[(s_se, 1)])
                    se_kvev[(ch, g, 0)] = C(s_se)
                    W("scalar", s_te, te_v[(ch, g)])
                    DO("scalar", lambda e, g=g, ch=ch, vb=vb: e.copy(
                        vn[:, (g * NKT + ch * 4) * 128:(g * NKT + ch * 4) * 128 + 512],
                        bank(vb)), inc=[(s_se, 1)])
                    se_kvev[(ch, g, 1)] = C(s_se)

            if maybe_stop("ph1b"):
                return
            W("gpsimd", s_te, te_xbfree[6])
            DO("gpsimd", lambda e: e.dma_start(
                x_own_bf[:, 0:8 * TOK].rearrange("p (h t) -> p h t", h=8),
                x_own[0:1024, :].rearrange("(h p) t -> p h t", p=128)),
               inc=[(s_xobf, 16)])
            W("gpsimd", s_te, te_xbfree[7])
            DO("gpsimd", lambda e: e.dma_start(
                x_own_bf[:, 8 * TOK:16 * TOK].rearrange("p (h t) -> p h t", h=8),
                x_own[1024:2048, :].rearrange("(h p) t -> p h t", p=128)),
               inc=[(s_xobf, 16)])
            xobf_tot = C(s_xobf)
            barrier()
            DO("sync", lambda e: e.dma_start(cosq_sb[:, :], cosq),
               inc=[(s_const, 16)])
            DO("sync", lambda e: e.dma_start(sinq_sb[:, :], sinq),
               inc=[(s_const, 16)])
            te_q, se_q, ve_q, ve_qn, te_qn_map = {}, {}, {}, {}, {}
            ve_kn, te_kn_map, ve_krope = {}, {}, {}
            se_kv_last = C(s_se)

            def emit_qn_pe(hq):
                W("tensor", s_ve, ve_q[hq])
                if hq >= 1:
                    W("tensor", s_ve, ve_qn[hq - 1])
                for half in range(2):
                    DO("tensor", lambda e, half=half, hq=hq: e.matmul(
                        PS[0:1, 2048 + half * 512:2048 + half * 512 + 512],
                        ones_bf[:, :],
                        sq_bf[hq % 4][:, half * 512:half * 512 + 512],
                        start=True, stop=True))
                te_qn_map[hq] = temark()

            def emit_qn_dve(hq):
                W("vector", s_te, te_qn_map[hq])
                DO("vector", lambda e: e.reduce_max(
                    sc1[0:1, 0:1], PS[0:1, 2048:3072], axis=AX.X))
                DR("vector")
                DO("vector", lambda e, hq=hq: e.tensor_copy(
                    qmax[0:1, hq:hq + 1], sc1[0:1, 0:1]))
                ve_qn[hq] = vemark()

            def emit_kn_pe(s):
                W("tensor", s_ve, ve_krope[s])
                if s >= 1:
                    W("tensor", s_ve, ve_kn[s - 1])
                for half in range(2):
                    DO("tensor", lambda e, half=half, s=s: e.matmul(
                        PS[0:1, 3072 + half * 512:3072 + half * 512 + 512],
                        ones_bf[:, :],
                        sqk_bf[s % 2][:, half * 512:half * 512 + 512],
                        start=True, stop=True))
                te_kn_map[s] = temark()

            def emit_kn_dve(s):
                W("vector", s_te, te_kn_map[s])
                DO("vector", lambda e: e.reduce_max(
                    sc1[0:1, 0:1], PS[0:1, 3072:4096], axis=AX.X))
                DR("vector")
                DO("vector", lambda e, s=s: e.tensor_tensor(
                    kmax[0:1, s // 4:s // 4 + 1], kmax[0:1, s // 4:s // 4 + 1],
                    sc1[0:1, 0:1], ALU.max))
                ve_kn[s] = vemark()

            for h in range(NH):
                sl = h % 2
                ks = h
                if h >= 2:
                    W("gpsimd", s_te, te_q[h - 2])
                DO("gpsimd", lambda e, h=h, sl=sl: e.dma_start(
                    wq_bf[sl][:, :].rearrange("p (hc c) -> p hc c", hc=DCH),
                    wq_r[h]), inc=[(s_wq[sl], 16)])
                W("tensor", s_wq[sl], C(s_wq[sl]))
                if h == 0:
                    W("tensor", s_xobf, xobf_tot)
                if h >= 2:
                    W("tensor", s_se, se_q[h - 2])
                for hc in range(DCH):
                    for half in range(2):
                        DO("tensor", lambda e, hc=hc, half=half, sl=sl: e.matmul(
                            PS[:, sl * 1024 + half * 512:sl * 1024 + half * 512 + 512],
                            wq_bf[sl][:, hc * 128:(hc + 1) * 128],
                            x_own_bf[:, hc * 1024 + half * 512:hc * 1024 + half * 512 + 512],
                            start=(hc == 0), stop=(hc == DCH - 1)))
                te_q[h] = temark()
                if h >= 3:
                    emit_qn_pe(h - 3)
                if h >= 1 and (h - 1) < 8:
                    emit_kn_pe(h - 1)
                W("scalar", s_te, te_q[h])
                if h >= 2:
                    W("scalar", s_ve, ve_q[h - 2])
                DO("scalar", lambda e, h=h, sl=sl: e.activation(
                    qf[sl][:, :], PS[:, sl * 1024:sl * 1024 + 1024],
                    AF.Identity, bias=bq_sb[:, h:h + 1]), inc=[(s_se, 1)])
                se_q[h] = C(s_se)
                W("sync", s_se, se_q[h])
                if h >= 2:
                    W("sync", s_ve, ve_q[h - 2])
                DO("sync", lambda e, sl=sl: e.dma_start(
                    qrot[sl][0:64, :], qf[sl][64:128, :]), inc=[(s_qr[sl], 16)])
                DO("sync", lambda e, sl=sl: e.dma_start(
                    qrot[sl][64:128, :], qf[sl][0:64, :]), inc=[(s_qr[sl], 16)])
                if ks < 8:
                    g, slc = ks // 4, ks % 4
                    cb = g * S + slc * 1024
                    if ks >= 2:
                        W("sync", s_ve, ve_krope[ks - 2])
                    DO("sync", lambda e, cb=cb, ks=ks: e.dma_start(
                        krot[ks % 2][0:64, :], ktb16[64:128, cb:cb + 1024]),
                       inc=[(s_kr, 16)])
                    DO("sync", lambda e, cb=cb, ks=ks: e.dma_start(
                        krot[ks % 2][64:128, :], ktb16[0:64, cb:cb + 1024]),
                       inc=[(s_kr, 16)])
                kr_cnt = C(s_kr)
                W("vector", s_qr[sl], C(s_qr[sl]))
                W("vector", s_se, se_q[h])
                if h == 0:
                    W("vector", s_const, C(s_const))
                DO("vector", lambda e, sl=sl: e.tensor_tensor(
                    t1_t[:, :], qf[sl][:, :], cosq_sb[:, :], ALU.mult))
                DO("vector", lambda e, sl=sl: e.tensor_tensor(
                    t2_t[:, :], qrot[sl][:, :], sinq_sb[:, :], ALU.mult))
                DR("vector")
                DO("vector", lambda e, h=h: e.tensor_tensor(
                    qt[:, h * TOK:(h + 1) * TOK], t1_t[:, :], t2_t[:, :],
                    ALU.add))
                DR("vector")
                if h >= 4:
                    W("vector", s_te, te_qn_map[h - 4])
                DO("vector", lambda e, h=h: e.tensor_tensor(
                    sq_bf[h % 4][:, :], qt[:, h * TOK:(h + 1) * TOK],
                    qt[:, h * TOK:(h + 1) * TOK], ALU.mult))
                ve_q[h] = vemark()
                if ks < 8:
                    g, slc = ks // 4, ks % 4
                    cb = g * S + slc * 1024
                    W("vector", s_kr, kr_cnt)
                    if h == 0:
                        W("vector", s_const, C(s_const))
                        W("vector", s_se, se_kv_last)
                    DO("vector", lambda e, cb=cb, slc=slc: e.tensor_tensor(
                        t1_t[:, :], ktb16[:, cb:cb + 1024],
                        cosk_sb[:, slc * 1024:slc * 1024 + 1024], ALU.mult))
                    DO("vector", lambda e, slc=slc, ks=ks: e.tensor_tensor(
                        t2_t[:, :], krot[ks % 2][:, :],
                        sink_sb[:, slc * 1024:slc * 1024 + 1024], ALU.mult))
                    DR("vector")
                    DO("vector", lambda e, cb=cb: e.tensor_tensor(
                        ktbf[:, cb:cb + 1024], t1_t[:, :], t2_t[:, :], ALU.add))
                    DR("vector")
                    if ks >= 2:
                        W("vector", s_te, te_kn_map[ks - 2])
                    DO("vector", lambda e, cb=cb, ks=ks: e.tensor_tensor(
                        sqk_bf[ks % 2][:, :], ktbf[:, cb:cb + 1024],
                        ktbf[:, cb:cb + 1024], ALU.mult))
                    ve_krope[ks] = vemark()
                if h >= 3:
                    emit_qn_dve(h - 3)
                if h >= 1 and (h - 1) < 8:
                    emit_kn_dve(h - 1)
            for hq in (NH - 3, NH - 2, NH - 1):
                emit_qn_pe(hq)
                emit_qn_dve(hq)

            DR("vector")
            for g in range(G):
                DO("vector", lambda e, g=g: e.tensor_scalar(
                    nct[0:1, g * 8:g * 8 + 8], qmax[0:1, g * 8:g * 8 + 8],
                    kmax[0:1, g:g + 1], None, ALU.mult))
            ve_prod = vemark()
            W("scalar", s_ve, ve_prod)
            DO("scalar", lambda e: e.activation(
                nct[0:1, 16:32], nct[0:1, 0:16], AF.Ln))
            DR("scalar")
            DO("scalar", lambda e: e.activation(
                nct[0:1, 32:48], nct[0:1, 16:32], AF.Exp, scale=0.5),
               inc=[(s_se, 1)])
            se_cst = C(s_se)
            W("vector", s_se, se_cst)
            DO("vector", lambda e: e.tensor_scalar(
                nct[0:1, 48:64], nct[0:1, 32:48], -(1.03 / 128.0), None,
                ALU.mult))
            ve_ncst = vemark()
            W("tensor", s_ve, ve_ncst)
            DO("tensor", lambda e: e.matmul(
                PS[:, 0:16], ones1f[:, :], nct[0:1, 48:64], start=True,
                stop=True))
            te_negc = temark()
            W("vector", s_te, te_negc)
            DO("vector", lambda e: e.tensor_copy(negc[:, :], PS[:, 0:16]))
            ve_negc = vemark()

            barrier()
            PL = 10

            def slot(kt):
                return kt if kt < 2 else 2 + (kt - 2) % 10

            ve_out = {}
            ve_invs_prev = 0
            for h in range(NH):
                g = h // 8
                te_qk, te_pv = {}, {}
                se0 = C(s_se)
                se_exp = {kt: se0 + kt + 1 for kt in range(NKT)}
                ve0 = C(s_ve)
                ve_rs = {kt: ve0 + kt // 4 + 1 for kt in range(NKT)}
                spair = (0, 1024, 3072)
                for kt in range(NKT + PL):
                    if kt < NKT:
                        if kt == 2 and h >= 1:
                            W("tensor", s_ve, ve_invs_prev)
                        if kt >= 3:
                            W("tensor", s_se, se_exp[kt - 3])
                        for half in range(2):
                            DO("tensor", lambda e, h=h, g=g, kt=kt, half=half: e.matmul(
                                PS[:, spair[kt % 3] + half * 512:spair[kt % 3] + half * 512 + 512],
                                ktbf[:, g * S + kt * 128:g * S + kt * 128 + 128],
                                qt[:, h * TOK + half * 512:h * TOK + half * 512 + 512],
                                start=True, stop=True))
                        te_qk[kt] = temark()
                    k2 = kt - PL
                    if k2 >= 0:
                        if kt >= NKT:
                            W("tensor", s_se, se_exp[k2])
                        if k2 == 0 and h >= 1:
                            W("tensor", s_ve, ve_out[h - 1])
                        for half in range(2):
                            DO("tensor", lambda e, g=g, k2=k2, half=half: e.matmul(
                                PS[:, 2048 + half * 512:2048 + half * 512 + 512],
                                vn[:, (g * NKT + k2) * 128:(g * NKT + k2) * 128 + 128],
                                et[slot(k2)][:, half * 512:half * 512 + 512],
                                start=(k2 == 0), stop=(k2 == NKT - 1)))
                        te_pv[k2] = temark()
                for kt in range(NKT):
                    W("scalar", s_te, te_pv[kt - PL] if kt >= PL else te_qk[kt])
                    if kt >= PL:
                        W("scalar", s_ve, ve_rs[kt - PL])
                    DO("scalar", lambda e, h=h, kt=kt: e.activation(
                        et[slot(kt)][:, :],
                        PS[:, spair[kt % 3]:spair[kt % 3] + 1024],
                        AF.Exp, bias=negc[:, h:h + 1], scale=1.0 / 128.0),
                       inc=[(s_se, 1)])
                    assert se_exp[kt] == C(s_se)
                for grp in range(8):
                    k0 = 4 * grp
                    W("vector", s_se, se_exp[k0 + 3])
                    DO("vector", lambda e, k0=k0: e.tensor_tensor(
                        tb1[:, :], et[slot(k0)][:, :], et[slot(k0 + 1)][:, :],
                        ALU.add))
                    DO("vector", lambda e, k0=k0: e.tensor_tensor(
                        tb2[:, :], et[slot(k0 + 2)][:, :],
                        et[slot(k0 + 3)][:, :], ALU.add))
                    DR("vector")
                    DO("vector", lambda e: e.tensor_tensor(
                        tb3[:, :], tb1[:, :], tb2[:, :], ALU.add))
                    DR("vector")
                    if grp == 0:
                        DO("vector", lambda e: e.tensor_copy(
                            rspart[:, :], tb3[:, :]))
                    else:
                        DO("vector", lambda e: e.tensor_tensor(
                            rspart[:, :], rspart[:, :], tb3[:, :], ALU.add))
                    DR("vector")
                    DO("vector", lambda e: e.engine_nop(), inc=[(s_ve, 1)])
                    assert ve_rs[k0 + 3] == C(s_ve)
                W("tensor", s_ve, ve_rs[NKT - 1])
                W("tensor", s_se, se_exp[NKT - 1])
                for half in range(2):
                    DO("tensor", lambda e, half=half: e.matmul(
                        PS[0:1, 3072 + half * 512:3072 + half * 512 + 512],
                        onesf[:, :], rspart[:, half * 512:half * 512 + 512],
                        start=True, stop=True))
                te_rs = temark()
                W("scalar", s_te, te_rs)
                DO("scalar", lambda e: e.activation(
                    rspart[0:1, 0:1024], PS[0:1, 3072:4096], AF.Ln))
                DR("scalar")
                DO("scalar", lambda e: e.activation(
                    invsb[0:1, 0:1024], rspart[0:1, 0:1024], AF.Exp,
                    scale=-1.0), inc=[(s_se, 1)])
                se_inv = C(s_se)
                W("tensor", s_se, se_inv)
                for half in range(2):
                    DO("tensor", lambda e, half=half: e.matmul(
                        PS[:, 3072 + half * 512:3072 + half * 512 + 512],
                        ones1f[:, :],
                        invsb[0:1, half * 512:half * 512 + 512],
                        start=True, stop=True))
                te_bc = temark()
                W("vector", s_te, te_bc)
                DO("vector", lambda e: e.tensor_copy(rspart[:, :], PS[:, 3072:4096]))
                ve_invs_prev = vemark()
                DR("vector")
                DO("vector", lambda e, h=h: e.tensor_tensor(
                    aoT[:, h * TOK:(h + 1) * TOK], PS[:, 2048:3072],
                    rspart[:, :], ALU.mult))
                ve_out[h] = vemark()

            barrier()
            te_wo, se_wo, ve_x1, te_st3 = {}, {}, {}, {}
            for dt in range(DCH):
                sl = dt % 2
                if dt >= 2:
                    W("gpsimd", s_te, te_wo[dt - 2])
                DO("gpsimd", lambda e, dt=dt, sl=sl: e.dma_start(
                    wo_bf[sl][:, :].rearrange("p (hc c) -> p hc c", hc=DCH),
                    wo_r[dt]), inc=[(s_wq[sl], 16)])
                if dt >= 2:
                    W("sync", s_ve, ve_x1[dt - 2])
                DO("sync", lambda e, dt=dt, sl=sl: e.dma_start(
                    xo_f[sl][:, :], x_own[dt * 128:(dt + 1) * 128, :]),
                   inc=[(s_xof[sl], 16)])
                W("tensor", s_wq[sl], C(s_wq[sl]))
                if dt >= 2:
                    W("tensor", s_se, se_wo[dt - 2])
                for hc in range(DCH):
                    for half in range(2):
                        DO("tensor", lambda e, hc=hc, half=half, sl=sl: e.matmul(
                            PS[:, sl * 1024 + half * 512:sl * 1024 + half * 512 + 512],
                            wo_bf[sl][:, hc * 128:(hc + 1) * 128],
                            aoT[:, hc * TOK + half * 512:hc * TOK + half * 512 + 512],
                            start=(hc == 0), stop=(hc == DCH - 1)))
                te_wo[dt] = temark()
                W("scalar", s_te, te_wo[dt])
                if dt >= 2:
                    W("scalar", s_ve, ve_x1[dt - 2])
                DO("scalar", lambda e, dt=dt, sl=sl: e.activation(
                    tmp_f[sl][:, :], PS[:, sl * 1024:sl * 1024 + 1024],
                    AF.Identity, bias=bo_sb[:, dt:dt + 1]), inc=[(s_se, 1)])
                se_wo[dt] = C(s_se)
                W("vector", s_se, se_wo[dt])
                W("vector", s_xof[sl], C(s_xof[sl]))
                DO("vector", lambda e, dt=dt, sl=sl: e.tensor_tensor(
                    x1[:, dt * TOK:(dt + 1) * TOK], tmp_f[sl][:, :],
                    xo_f[sl][:, :], ALU.add))
                DR("vector")
                if dt >= 2:
                    W("vector", s_te, te_st3[dt - 2])
                DO("vector", lambda e, dt=dt, sl=sl: e.tensor_tensor(
                    sq_t[sl][:, :], x1[:, dt * TOK:(dt + 1) * TOK],
                    x1[:, dt * TOK:(dt + 1) * TOK], ALU.mult))
                DR("vector")
                DO("vector", lambda e, dt=dt, sl=sl: e.tensor_copy(
                    xc_bf[sl][:, :], x1[:, dt * TOK:(dt + 1) * TOK]))
                ve_x1[dt] = vemark()

                def emit_st3(dq):
                    sq = dq % 2
                    W("tensor", s_ve, ve_x1[dq])
                    for half in range(2):
                        DO("tensor", lambda e, sq=sq, half=half, dq=dq: e.matmul(
                            PS[0:1, 2048 + half * 512:2048 + half * 512 + 512],
                            ones_bf[:, :],
                            xc_bf[sq][:, half * 512:half * 512 + 512],
                            start=(dq == 0), stop=(dq == DCH - 1)))
                    for half in range(2):
                        DO("tensor", lambda e, sq=sq, half=half, dq=dq: e.matmul(
                            PS[0:1, 3072 + half * 512:3072 + half * 512 + 512],
                            ones_bf[:, :], sq_t[sq][:, half * 512:half * 512 + 512],
                            start=(dq == 0), stop=(dq == DCH - 1)))
                    te_st3[dq] = temark()

                if dt >= 1:
                    emit_st3(dt - 1)
                if dt == DCH - 1:
                    emit_st3(dt)

            def ln_tail(lo=2048):
                W("vector", s_te, C(s_te))
                DO("vector", lambda e, lo=lo: e.tensor_scalar(
                    lt1[0:1, :], PS[0:1, lo:lo + 1024], 1.0 / float(D),
                    None, ALU.mult))
                DO("vector", lambda e, lo=lo: e.tensor_scalar(
                    lt2[0:1, :], PS[0:1, lo + 1024:lo + 2048], 1.0 / float(D),
                    None, ALU.mult))
                DR("vector")
                DO("vector", lambda e: e.tensor_tensor(
                    lt3[0:1, :], lt1[0:1, :], lt1[0:1, :],
                    ALU.mult))
                DR("vector")
                DO("vector", lambda e: e.tensor_tensor(
                    lt2[0:1, :], lt2[0:1, :], lt3[0:1, :],
                    ALU.subtract))
                vemark()
                W("scalar", s_ve, C(s_ve))
                DO("scalar", lambda e: e.activation(
                    lt3[0:1, :], lt2[0:1, :], AF.Ln,
                    bias=epsv[0:1, 0:1]))
                DR("scalar")
                DO("scalar", lambda e: e.activation(
                    lt2[0:1, :], lt3[0:1, :], AF.Exp, scale=-0.5),
                   inc=[(s_se, 1)])
                W("vector", s_se, C(s_se))
                DO("vector", lambda e: e.tensor_tensor(
                    lt1[0:1, :], lt1[0:1, :], lt2[0:1, :],
                    ALU.mult))
                DR("vector")
                DO("vector", lambda e: e.tensor_scalar(
                    lt1[0:1, :], lt1[0:1, :], -1.0, None, ALU.mult))
                vemark()
                W("tensor", s_ve, C(s_ve))
                for half in range(2):
                    DO("tensor", lambda e, half=half: e.matmul(
                        PS[:, half * 512:half * 512 + 512], ones1f[:, :],
                        lt2[0:1, half * 512:half * 512 + 512],
                        start=True, stop=True))
                for half in range(2):
                    DO("tensor", lambda e, half=half: e.matmul(
                        PS[:, 1024 + half * 512:1024 + half * 512 + 512],
                        ones1f[:, :], lt1[0:1, half * 512:half * 512 + 512],
                        start=True, stop=True))
                temark()
                W("vector", s_te, C(s_te))
                DO("vector", lambda e: e.tensor_copy(a_sb[:, :], PS[:, 0:1024]))
                DO("vector", lambda e: e.tensor_copy(b_sb[:, :], PS[:, 1024:2048]))
                vemark()

            ln_tail()
            ve_ln1 = {}
            for dt in range(DCH):
                sl = dt % 2
                DO("vector", lambda e, dt=dt, sl=sl: e.tensor_tensor(
                    tmp_f[sl][:, :], x1[:, dt * TOK:(dt + 1) * TOK], a_sb[:, :],
                    ALU.mult))
                DR("vector")
                DO("vector", lambda e, dt=dt, sl=sl: e.tensor_tensor(
                    x1[:, dt * TOK:(dt + 1) * TOK], tmp_f[sl][:, :], b_sb[:, :],
                    ALU.add))
                DR("vector")
                DO("vector", lambda e, dt=dt: e.tensor_copy(
                    x1b[:, dt * TOK:(dt + 1) * TOK],
                    x1[:, dt * TOK:(dt + 1) * TOK]))
                ve_ln1[dt] = vemark()
                W("tensor", s_ve, ve_ln1[dt])
                for half in range(2):
                    DO("tensor", lambda e, dt=dt, half=half: e.matmul(
                        PS[0:16, half * 512:half * 512 + 512],
                        gwt_sb[:, dt * 16:(dt + 1) * 16],
                        x1[:, dt * TOK + half * 512:dt * TOK + half * 512 + 512],
                        start=(dt == 0), stop=(dt == DCH - 1)))
            te_lg_pre = temark()

            if maybe_stop("ph3"):
                return
            barrier()
            state = {"sg_last": 0, "h_last": 0, "ce_last": 0, "y_last": {},
                     "slabfree": {}, "dslabfree": {}, "y_gate": 0,
                     "comb_last": 0, "ln2sq": {}}

            W("vector", s_te, te_lg_pre)
            DO("vector", lambda e: e.tensor_copy(lg_sb[:, :], PS[0:16, 0:1024]))
            ve_lg = vemark()
            ve_lgt = {}
            te_ftr = {}
            for tt in range(8):
                pb = 2 + tt % 2
                if tt == 0:
                    W("tensor", s_ve, ve_lg)
                if tt >= 2:
                    W("tensor", s_ve, ve_lgt[tt - 2])
                DO("tensor", lambda e, tt=tt, pb=pb: e.transpose(
                    PS[:, pb * 512:pb * 512 + 16],
                    lg_sb[0:16, tt * 128:(tt + 1) * 128], idf_sb[0:16, 0:16]))
                te_ftr[tt] = temark()
                W("vector", s_te, te_ftr[tt])
                DO("vector", lambda e, tt=tt, pb=pb: e.tensor_copy(
                    lgt8[:, tt * 16:(tt + 1) * 16], PS[:, pb * 512:pb * 512 + 16]))
                ve_lgt[tt] = vemark()
            state["h_last"] = ve_lgt[7]
            state["sg_last"] = C(s_se)

            ve_nm1 = {}
            ve_cmb = {}
            se_rexp = {}
            te_rtr = {}

            def hook_scalar(j):
                tt = j - 1
                if 0 <= tt < 8:
                    W("scalar", s_ve, ve_nm1[tt])
                    if tt >= 1:
                        W("scalar", s_ve, ve_cmb[tt - 1])
                    DO("scalar", lambda e, tt=tt: e.activation(
                        esc_t[:, :], lgt8[:, tt * 16:(tt + 1) * 16], AF.Exp,
                        bias=nm18[:, tt * 16:tt * 16 + 1]), inc=[(s_se, 1)])
                    se_rexp[tt] = C(s_se)

            def hook_vector(j):
                tt = j
                if tt < 8:
                    DO("vector", lambda e, tt=tt: e.reduce_max(
                        m1c[:, :], lgt8[:, tt * 16:(tt + 1) * 16], axis=AX.X))
                    DR("vector")
                    DO("vector", lambda e, tt=tt: e.tensor_scalar(
                        nm18[:, tt * 16:tt * 16 + 1], m1c[:, :], -1.0, None,
                        ALU.mult))
                    ve_nm1[tt] = vemark()
                tt = j - 1
                if 0 <= tt < 8:
                    W("vector", s_se, se_rexp[tt])
                    DO("vector", lambda e: e.reduce_sum(rsum_t[:, :], esc_t[:, :],
                                                        axis=AX.X))
                    DR("vector")
                    DO("vector", lambda e: e.reciprocal(rinv_t[:, :], rsum_t[:, :]))
                    DR("vector")
                    DO("vector", lambda e: e.tensor_scalar(
                        sc_t[:, :], esc_t[:, :], rinv_t[:, :], None, ALU.mult))
                    DR("vector")
                    DO("vector", lambda e: e.reduce_max(m1c[:, :], sc_t[:, :],
                                                        axis=AX.X))
                    DR("vector")
                    DO("vector", lambda e: e.tensor_scalar(
                        eqm_t[:, :], sc_t[:, :], m1c[:, :], None, ALU.is_equal))
                    DR("vector")
                    DO("vector", lambda e: e.tensor_scalar(
                        big_t[:, :], eqm_t[:, :], 1e30, None, ALU.mult))
                    DR("vector")
                    DO("vector", lambda e: e.tensor_tensor(
                        msk_t[:, :], sc_t[:, :], big_t[:, :], ALU.subtract))
                    DR("vector")
                    DO("vector", lambda e: e.reduce_max(m2c[:, :], msk_t[:, :],
                                                        axis=AX.X))
                    DR("vector")
                    DO("vector", lambda e: e.tensor_scalar(
                        keep_t[:, :], sc_t[:, :], m2c[:, :], None, ALU.is_ge))
                    DR("vector")
                    DO("vector", lambda e, tt=tt: e.tensor_tensor(
                        cmb8[:, tt * 16:(tt + 1) * 16], sc_t[:, :], keep_t[:, :],
                        ALU.mult))
                    ve_cmb[tt] = vemark()

            def hook_tensor_post(j):
                tt = j - 2
                if 0 <= tt < 8:
                    W("tensor", s_ve, ve_cmb[tt])
                    DO("tensor", lambda e, tt=tt: e.transpose(
                        PS[0:16, 2048:2176], cmb8[:, tt * 16:(tt + 1) * 16],
                        idf_sb[:, :]))
                    te_rtr[tt] = temark()

            def hook_vector_post(j):
                tt = j - 2
                if 0 <= tt < 8:
                    W("vector", s_te, te_rtr[tt])
                    DO("vector", lambda e, tt=tt: e.tensor_copy(
                        comb_sb[0:16, tt * 128:(tt + 1) * 128],
                        PS[0:16, 2048:2176]))
                    state["comb_last"] = vemark()
                    state["y_gate"] = state["comb_last"]

            def gu_block(tag, slab_src, use_ce, hooks=False):
                ve_h_tiles = {}
                for j in range(NJ):
                    sl = j % 2
                    key = (tag, j - 2)
                    if key in state["slabfree"]:
                        W("gpsimd", s_te, state["slabfree"][key])
                    DO("gpsimd", lambda e, j=j, sl=sl: e.dma_start(
                        wgu_bf[sl][:, :].rearrange("p (u f) -> p u f", u=2),
                        slab_src(j)), inc=[(s_wg[sl], 16)])
                    W("tensor", s_wg[sl], C(s_wg[sl]))
                    W("tensor", s_se, state["sg_last"])
                    for hc in range(DCH):
                        for half in range(2):
                            DO("tensor", lambda e, hc=hc, half=half, sl=sl: e.matmul(
                                PS[:, half * 512:half * 512 + 512],
                                wgu_bf[sl][:, hc * 128:(hc + 1) * 128],
                                x1b[:, hc * TOK + half * 512:hc * TOK + half * 512 + 512],
                                start=(hc == 0), stop=(hc == DCH - 1)))
                    te_g = temark()
                    W("tensor", s_ve, state["h_last"])
                    if use_ce:
                        W("tensor", s_ve, state["ce_last"])
                    for hc in range(DCH):
                        for half in range(2):
                            DO("tensor", lambda e, hc=hc, half=half, sl=sl: e.matmul(
                                PS[:, 1024 + half * 512:1024 + half * 512 + 512],
                                wgu_bf[sl][:, D + hc * 128:D + hc * 128 + 128],
                                x1b[:, hc * TOK + half * 512:hc * TOK + half * 512 + 512],
                                start=(hc == 0), stop=(hc == DCH - 1)))
                    te_u = temark()
                    state["slabfree"][(tag, j)] = te_u
                    if hooks:
                        hook_scalar(j)
                    W("scalar", s_te, te_g)
                    if j >= 2:
                        W("scalar", s_ve, ve_h_tiles[j - 2])
                    DO("scalar", lambda e, sl=sl: e.activation(
                        sg_bf[sl][:, :], PS[:, 0:1024], AF.Silu),
                       inc=[(s_se, 1)])
                    state["sg_last"] = C(s_se)
                    W("vector", s_te, te_u)
                    if use_ce:
                        DO("vector", lambda e, sl=sl: e.tensor_tensor(
                            uf_sb[sl][:, :], PS[:, 1024:2048], ce_sb[:, :],
                            ALU.mult))
                    else:
                        DO("vector", lambda e, sl=sl: e.tensor_copy(
                            uf_sb[sl][:, :], PS[:, 1024:2048]))
                    DR("vector")
                    W("vector", s_se, state["sg_last"])
                    DO("vector", lambda e, j=j, sl=sl: e.tensor_tensor(
                        hpr[:, j * TOK:(j + 1) * TOK], sg_bf[sl][:, :],
                        uf_sb[sl][:, :], ALU.mult))
                    ve_h_tiles[j] = vemark()
                    state["h_last"] = ve_h_tiles[j]
                    if hooks:
                        hook_vector(j)
                        hook_tensor_post(j)
                        hook_vector_post(j)

            te_ln2st = {}

            def emit_ln2_stats(dq):
                sq = dq % 2
                W("tensor", s_ve, state["ln2sq"][dq])
                for half in range(2):
                    DO("tensor", lambda e, sq=sq, half=half, dq=dq: e.matmul(
                        PS[0:1, half * 512:half * 512 + 512],
                        ones_bf[:, :],
                        xc_bf[sq][:, half * 512:half * 512 + 512],
                        start=(dq == 0), stop=(dq == DCH - 1)))
                for half in range(2):
                    DO("tensor", lambda e, sq=sq, half=half, dq=dq: e.matmul(
                        PS[0:1, 1024 + half * 512:1024 + half * 512 + 512],
                        ones_bf[:, :], sq7[sq][:, half * 512:half * 512 + 512],
                        start=(dq == 0), stop=(dq == DCH - 1)))
                te_ln2st[dq] = temark()

            def down_block(tag, slab_src, stats=False):
                h_ready = state["h_last"]
                for hp in range(8):
                    sl = hp % 2
                    key = (tag, hp - 2)
                    if key in state["dslabfree"]:
                        W("gpsimd", s_te, state["dslabfree"][key])
                    DO("gpsimd", lambda e, hp=hp, sl=sl: e.dma_start(
                        wd_bf[sl][:, :].rearrange("p (u f) -> p u f", u=2),
                        slab_src(hp)), inc=[(s_wd[sl], 16)])
                    W("tensor", s_wd[sl], C(s_wd[sl]))
                    if hp == 0:
                        W("tensor", s_ve, h_ready)
                        W("tensor", s_ve, state["y_gate"])
                    for tp in range(2):
                        ht = hp * 2 + tp
                        yb = 4 + (ht % 2) * 2
                        if ht >= 2:
                            W("tensor", s_ve, state["y_last"][(tag, ht - 2)])
                        for ic in range(NJ):
                            for half in range(2):
                                DO("tensor", lambda e, tp=tp, ic=ic, half=half, sl=sl, yb=yb: e.matmul(
                                    PS[:, yb * 512 + half * 512:yb * 512 + half * 512 + 512],
                                    wd_bf[sl][:, tp * NJ * 128 + ic * 128:tp * NJ * 128 + ic * 128 + 128],
                                    hpr[:, ic * TOK + half * 512:ic * TOK + half * 512 + 512],
                                    start=(ic == 0), stop=(ic == NJ - 1)))
                        te_y = temark()
                        W("vector", s_te, te_y)
                        DO("vector", lambda e, ht=ht, yb=yb: e.tensor_tensor(
                            x1[:, ht * TOK:(ht + 1) * TOK],
                            PS[:, yb * 512:yb * 512 + 1024],
                            x1[:, ht * TOK:(ht + 1) * TOK], ALU.add))
                        state["y_last"][(tag, ht)] = vemark()
                        if stats:
                            sl7 = ht % 2
                            if ht >= 2:
                                W("vector", s_te, te_ln2st[ht - 2])
                            DO("vector", lambda e, ht=ht, sl7=sl7: e.tensor_tensor(
                                sq7[sl7][:, :], x1[:, ht * TOK:(ht + 1) * TOK],
                                x1[:, ht * TOK:(ht + 1) * TOK], ALU.mult))
                            DR("vector")
                            DO("vector", lambda e, ht=ht, sl7=sl7: e.tensor_copy(
                                xc_bf[sl7][:, :], x1[:, ht * TOK:(ht + 1) * TOK]))
                            state["ln2sq"][ht] = vemark()
                            if ht >= 1:
                                emit_ln2_stats(ht - 1)
                            if ht == DCH - 1:
                                emit_ln2_stats(ht)
                    state["dslabfree"][(tag, hp)] = C(s_te)

            for r in range(2):
                tag = "sh%d" % r
                gu_block(tag,
                         lambda j, r=r: sgu_r[r * NJ + j].rearrange(
                             "u p h c -> p u (h c)"),
                         use_ce=False, hooks=(r == 0))
                down_block(tag,
                           lambda hp, r=r: sd_r[hp][:, :, r * NJ:(r + 1) * NJ, :]
                           .rearrange("u p i c -> p u (i c)"))

            for ex in range(NE):
                tag = "e%d" % ex
                W("tensor", s_ve, state["h_last"])
                W("tensor", s_ve, state["comb_last"])
                for half in range(2):
                    DO("tensor", lambda e, ex=ex, half=half: e.matmul(
                        PS[:, 1024 + half * 512:1024 + half * 512 + 512],
                        sel_sb[:, ex * 128:(ex + 1) * 128],
                        comb_sb[0:16, half * 512:half * 512 + 512],
                        start=True, stop=True))
                te_ce = temark()
                W("vector", s_te, te_ce)
                DO("vector", lambda e: e.tensor_copy(ce_sb[:, :], PS[:, 1024:2048]))
                state["ce_last"] = vemark()
                gu_block(tag,
                         lambda j, ex=ex: wgu_r[ex, j].rearrange(
                             "u p h c -> p u (h c)"),
                         use_ce=True)
                down_block(tag,
                           lambda hp, ex=ex: wd_r[ex, hp].rearrange(
                               "u p i c -> p u (i c)"),
                           stats=(ex == NE - 1))

            if maybe_stop("ph6"):
                return
            barrier()
            ln_tail(0)
            for dt in range(DCH):
                sl = dt % 2
                DO("vector", lambda e, dt=dt, sl=sl: e.tensor_tensor(
                    tmp7[sl][:, :], x1[:, dt * TOK:(dt + 1) * TOK], a_sb[:, :],
                    ALU.mult))
                DR("vector")
                DO("vector", lambda e, dt=dt, sl=sl: e.tensor_tensor(
                    x1[:, dt * TOK:(dt + 1) * TOK], tmp7[sl][:, :], b_sb[:, :],
                    ALU.add))
                ve_n7 = vemark()
                W("sync", s_ve, ve_n7)
                DO("sync", lambda e, dt=dt: e.dma_start(
                    out[dt * 128:(dt + 1) * 128, :],
                    x1[:, dt * TOK:(dt + 1) * TOK]), inc=[(s_st, 16)])
            W("sync", s_st, C(s_st))

        for name in ENGINES:
            def body(eng, name=name):
                pp.reset(name, eng)
                emit(pp)
            getattr(blk, name)(body)

    return nc


def _prep(inputs):
    f32 = np.float32
    c = np.ascontiguousarray
    x = np.asarray(inputs["L0_attn_x"], f32)
    wq = np.asarray(inputs["L0_attn_wq"], f32)
    bq = np.asarray(inputs["L0_attn_bq"], f32)
    wk = np.asarray(inputs["L0_attn_wk"], f32)
    bk = np.asarray(inputs["L0_attn_bk"], f32)
    wv = np.asarray(inputs["L0_attn_wv"], f32)
    bv = np.asarray(inputs["L0_attn_bv"], f32)
    wo = np.asarray(inputs["L0_attn_wo"], f32)
    bo = np.asarray(inputs["L0_attn_bo"], f32)
    gate_w = np.asarray(inputs["L0_ffn_gate_w"], f32)
    wgu = np.asarray(inputs["L0_ffn_w_gate_up"], f32)
    wd = np.asarray(inputs["L0_ffn_w_down"], f32)
    sgu = np.asarray(inputs["L0_ffn_shared_gate_up"], f32)
    sd = np.asarray(inputs["L0_ffn_shared_down"], f32)

    xT = c(x.transpose(0, 2, 1))
    wq_r = c(wq.reshape(DCH, 128, NH, 128).transpose(2, 1, 0, 3))
    wk_r = c(wk.reshape(DCH, 128, G, 128).transpose(2, 1, 0, 3))
    wv_r = c(wv.reshape(DCH, 128, G, 128).transpose(2, 1, 0, 3))
    wo_r = c(wo.reshape(DCH, 128, DCH, 128).transpose(2, 1, 0, 3))
    bq_r = c(bq.reshape(NH, 128).T)
    bk_r = c(bk.reshape(G, 128).T)
    bv_row = c(bv.reshape(1, G * 128))
    bo_r = c(bo.reshape(DCH, 128).T)
    gwt = c(gate_w.T)
    wg5 = wgu.reshape(NE, DCH, 128, 2 * NJ, 128).transpose(0, 3, 2, 1, 4)
    wgu_r = c(np.stack([wg5[:, :NJ], wg5[:, NJ:]], axis=2))
    wd5 = wd.reshape(NE, NJ, 128, DCH, 128).transpose(0, 3, 2, 1, 4)
    wd_r = c(wd5.reshape(NE, 8, 2, 128, NJ, 128))
    sg4 = sgu.reshape(DCH, 128, 2 * SJ, 128).transpose(2, 1, 0, 3)
    sgu_r = c(np.stack([sg4[:SJ], sg4[SJ:]], axis=1))
    sd4 = sd.reshape(SJ, 128, DCH, 128).transpose(2, 1, 0, 3)
    sd_r = c(sd4.reshape(8, 2, 128, SJ, 128))

    pos = np.arange(S, dtype=np.float64)
    invf = 1.0 / (10000.0 ** (np.arange(0, HD, 2, dtype=np.float64) / HD))
    fr = np.outer(pos, invf)
    cos_t = np.cos(fr).T
    sin_t = np.sin(fr).T
    bf16 = ml_dtypes.bfloat16
    cosk = np.concatenate([cos_t, cos_t], axis=0).astype(bf16)
    sink = np.concatenate([-sin_t, sin_t], axis=0).astype(bf16)
    idf = np.eye(128, dtype=f32)
    selm = np.zeros((NE, NE * 128), f32)
    for e in range(NE):
        selm[e, e * 128:(e + 1) * 128] = 1.0

    shared = dict(wq_r=wq_r, wk_r=wk_r, wv_r=wv_r, wo_r=wo_r, bq_r=bq_r,
                  bk_r=bk_r, bv_row=bv_row, bo_r=bo_r, gwt=gwt, wgu_r=wgu_r,
                  wd_r=wd_r, sgu_r=sgu_r, sd_r=sd_r, cosk=cosk, sink=sink,
                  idf=idf, sel=selm)
    in_maps = []
    for core in range(8):
        b, qc = core // 4, core % 4
        m = dict(shared)
        m["x_batch"] = xT[b]
        m["x_own"] = c(xT[b][:, qc * TOK:(qc + 1) * TOK])
        m["cosq"] = c(cosk[:, qc * TOK:(qc + 1) * TOK])
        m["sinq"] = c(sink[:, qc * TOK:(qc + 1) * TOK])
        in_maps.append(m)
    return in_maps


def kernel(**inputs):
    in_maps = _prep(inputs)
    nc = build_program()
    res = run_bass_kernel_spmd(nc, in_maps, core_ids=list(range(8)))
    out = np.empty((2, S, D), np.float32)
    for core in range(8):
        b, qc = core // 4, core % 4
        out[b, qc * TOK:(qc + 1) * TOK, :] = res.results[core]["out"].T
    return out
